# version 18
# baseline (speedup 1.0000x reference)
import numpy as np
from contextlib import ExitStack

import concourse.bass as bass
import concourse.mybir as mybir
from concourse.bass_utils import run_bass_kernel_spmd

F32 = mybir.dt.float32
BF16 = mybir.dt.bfloat16
AF = mybir.ActivationFunctionType
ALU = mybir.AluOpType

D, DT = 1024, 8
KD, KT = 4096, 32
HD, HT = 2816, 22
EPS_LN = 1e-5
EPS_WKV = 1e-8

GROUPS_D = [(0, 8, [64])]
GROUPS_H = [(4 * i, 4, [64, 32]) for i in range(5)] + [(20, 2, [64, 32, 16])]

SH_PLAIN = [("wk", 1, D), ("wv", 1, D), ("wr", 1, D), ("wo", 1, D),
            ("wfr", 1, D), ("wfk", 1, KD), ("wfv", 4, D)]
SH_OFF = {}
_o = 0
for _n, _b, _c in SH_PLAIN:
    SH_OFF[_n] = _o
    _o += _b * _c
SH_Q = [("wgt", DT, HD, 352), ("wut", DT, HD, 352), ("wdt", HT, D, 128)]
for _n, _kt, _cols, _w in SH_Q:
    SH_OFF[_n] = _o
    _o += _kt * _w
SHW = _o

PAR_FIELDS = ["ln1g", "ln1b", "ln2g", "ln2b", "mixk", "mixv", "mixr",
              "tdecay", "tfirst", "flfirst", "flcarry", "xprev", "xprev2"]
PAR = {n: i * DT for i, n in enumerate(PAR_FIELDS)}
NPAR = len(PAR_FIELDS) * DT


def _perm_d():
    pd = np.empty(D, np.int64)
    for q in range(D):
        tau, p = q // 128, q % 128
        pd[q] = 16 * (p % 64) + 8 * (p // 64) + tau
    return pd


def _perm_h():
    ph = np.empty(HD, np.int64)
    for G in range(5):
        for gp in range(32):
            for ip in range(4):
                for tp in range(4):
                    ph[(4 * G + tp) * 128 + gp + 32 * ip] = (
                        512 * G + 16 * gp + 4 * ip + tp)
    for gp in range(16):
        for ip in range(8):
            for tp in range(2):
                ph[(20 + tp) * 128 + gp + 16 * ip] = 2560 + 16 * gp + 2 * ip + tp
    return ph


PD = _perm_d()
PH = _perm_h()


LABELS = {}
TAG = [""]
RQNC = [None]


class RQ:

    def __init__(self, name, serialize=True):
        self.name, self.n = name, 0
        self.serialize = serialize
        self.ops, self._pend = [], []
        self.sem = None

    def wait(self, q, cnt):
        if cnt > 0:
            self._pend.append((q, cnt))

    def emit(self, fn, inc=1):
        self.ops.append((self._pend, fn, inc, TAG[0]))
        self._pend = []
        self.n += inc
        return self.n

    def replay(self, eng):
        prev = 0
        for waits, fn, inc, tag in self.ops:
            for q, cnt in waits:
                eng.wait_ge(q.sem, cnt)
            if self.serialize and prev > 0:
                eng.wait_ge(self.sem, prev)
            i0 = RQNC[0].n_instructions() if RQNC[0] is not None else None
            inst = fn(eng)
            if i0 is not None:
                for k in range(i0, RQNC[0].n_instructions()):
                    LABELS[f"I-{k}"] = tag
            if inc:
                inst.then_inc(self.sem, inc)
            prev += inc


class Stream:

    def __init__(self, name):
        self.name, self.n = name, 0
        self.sem = None


class DmaEng:

    def __init__(self, name):
        self.name = name
        self.ops, self._pend = [], []

    def wait(self, q, cnt):
        if cnt > 0:
            self._pend.append((q, cnt))

    def dma(self, stream, out, in_, **kw):
        self.ops.append((self._pend, ("dma", stream, out, in_, kw)))
        self._pend = []
        stream.n += 16
        return stream.n

    def raw(self, fn, sem=None):
        self.ops.append((self._pend, ("raw", fn, sem)))
        self._pend = []

    def replay(self, eng):
        for waits, op in self.ops:
            for q, cnt in waits:
                eng.wait_ge(q.sem, cnt)
            if op[0] == "dma":
                _, stream, out, in_, kw = op
                eng.dma_start(out=out, in_=in_, **kw).then_inc(stream.sem, 16)
            else:
                _, fn, sem = op
                inst = fn(eng)
                if sem is not None:
                    inst.then_inc(sem.sem)


class SemRef:

    def __init__(self):
        self.sem = None


def build_program(tokens=4096, T=512):
    assert tokens % T == 0
    NCH = tokens // T
    nc = bass.Bass("TRN2", target_bir_lowering=False)
    ctx = ExitStack()

    xT = nc.dram_tensor("xT", [D, tokens], F32, kind="ExternalInput").ap()
    par_d = nc.dram_tensor("par", [128, NPAR], F32, kind="ExternalInput").ap()
    gt0_d = nc.dram_tensor("gate0", [128, 16], F32, kind="ExternalInput").ap()
    wsh = nc.dram_tensor("wsh", [128, SHW], F32, kind="ExternalInput").ap()
    outT = nc.dram_tensor("outT", [D, tokens], F32, kind="ExternalOutput").ap()
    wsc = {}
    for name, kt, cols in [("wk", DT, D), ("wv", DT, D), ("wr", DT, D),
                           ("wo", DT, D), ("wfk", DT, KD), ("wfv", KT, D),
                           ("wfr", DT, D), ("wgt", DT, HD), ("wut", DT, HD),
                           ("wdt", HT, D)]:
        wsc[name] = nc.dram_tensor("sc_" + name, [128, kt * cols], BF16).ap()
    snd = nc.dram_tensor("cc_snd", [128, 16], F32)
    rcv = nc.dram_tensor("cc_rcv", [256, 16], F32)
    AG1W = 2048
    snd_w1 = nc.dram_tensor("cc_sndw1", [128, AG1W], BF16)
    rcv_w1 = nc.dram_tensor("cc_rcvw1", [1024, AG1W], BF16, addr_space="Shared")
    snd_w2 = nc.dram_tensor("cc_sndw2", [128, SHW - AG1W], BF16)
    rcv_w2 = nc.dram_tensor("cc_rcvw2", [1024, SHW - AG1W], BF16,
                            addr_space="Shared")

    arena0 = (nc.sbuf_base + 31) & ~31
    off = [arena0]

    def fresh(name, shape, dtype):
        t = nc.alloc_sbuf_tensor_at(name, shape, dtype, offset=off[0])
        sz = int(np.prod(shape[1:])) * (4 if dtype == F32 else 2)
        off[0] += (sz + 31) & ~31
        return t

    def at(name, shape, dtype, o):
        return nc.alloc_sbuf_tensor_at(name, shape, dtype, offset=o)

    def ofs(t):
        return t.manual_sbuf_range[0]

    PARS = fresh("PARS", [128, NPAR], F32)
    GT0 = fresh("GT0", [128, 16], F32)
    AUXG = fresh("AUXG", [128, 16], F32)
    AUX = fresh("AUX", [128, 96], F32)
    ONESC = fresh("ONESC", [128, 1], BF16)
    SMALL = fresh("SMALL", [1, 3 * T + 32], F32)
    SMB = fresh("SMB", [1, 2 * T], BF16)
    ONESR = fresh("ONESR", [1, 128], BF16)
    HALO = fresh("HALO", [128, 7 * DT], BF16)
    CONST_VALS = [EPS_LN, EPS_WKV, -0.5, -0.25, -0.75, -1.25, -1.75, -2.5,
                  -3.5, -5.0, 0.5]
    CONSTS = fresh("CONSTS", [128, len(CONST_VALS)], F32)
    WB = [fresh("WB0", [128, 8192], BF16), fresh("WB1", [128, 8192], BF16)]
    trans0 = off[0]
    X = fresh("X", [128, DT * T], F32)
    X1 = fresh("X1", [128, DT * T], F32)
    XN = fresh("XN", [128, DT * (T + 1)], BF16)
    MIX = fresh("MIX", [128, DT * T], BF16)
    MIX2 = fresh("MIX2", [128, DT * T], BF16)
    DDb = fresh("DD", [128, DT * T], BF16)
    EK = fresh("EK", [128, DT * T], BF16)
    VV = fresh("VV", [128, DT * T], BF16)
    UU = fresh("UU", [128, DT * T], BF16)
    A8 = fresh("A8", [128, DT * T], BF16)
    B8 = fresh("B8", [128, DT * T], BF16)
    HH = fresh("HH", [128, HT * T], BF16)
    LQ1 = fresh("LQ1", [128, 2 * T], F32)
    LQ2 = fresh("LQ2", [128, 2 * T], BF16)
    LQ3 = fresh("LQ3", [128, 2 * T], BF16)
    LQ4 = fresh("LQ4", [128, 2 * T], BF16)
    AXG = fresh("AXG", [128, DT * T], F32)
    KRL = fresh("KRL", [128, 2 * T], BF16)
    TMQ1 = fresh("TMQ1", [128, T], F32)
    TMQ2 = fresh("TMQ2", [128, T], F32)
    RBq = fresh("RBq", [128, T], F32)
    SSq = fresh("SSq", [128, T], BF16)
    SCRA = fresh("SCRA", [128, T], F32)
    SCRB = fresh("SCRB", [128, T], F32)
    for _ci, _cv in enumerate(CONST_VALS):
        nc.const_aps.register(F32, _cv, CONSTS[:, _ci:_ci + 1])


    CSQ = at("CSQ", [128, DT * T], BF16, ofs(DDb))
    CXB = at("CXB", [128, DT * T], BF16, ofs(EK))
    KK = at("KK", [128, KT * T], BF16, ofs(A8))
    X3 = at("X3", [128, DT * T], F32, ofs(VV))
    WKV = at("WKV", [128, DT * T], BF16, ofs(DDb))
    RWKV = at("RWKV", [128, DT * T], BF16, ofs(MIX))
    RBW = at("RBW", [128, DT * T], BF16, ofs(VV))
    Rb = at("Rb", [128, DT * T], BF16, ofs(MIX2))
    YMIX = at("YMIX", [128, DT * T], BF16, ofs(UU))
    RRb = at("RRb", [128, DT * T], BF16, ofs(MIX))
    XQ = at("XQ", [128, DT * T], BF16, ofs(MIX2))
    HQ = at("HQ", [128, HT * T], BF16, ofs(HH))
    Fs = at("Fs", [128, DT * T], F32, ofs(A8))
    OUTb = at("OUTb", [128, DT * T], F32, ofs(VV))
    TMPF = at("TMPF", [128, DT * T], F32, ofs(MIX))
    SG_CH = [at(f"SG{i}", [128, 2 * T], BF16, o) for i, o in enumerate(
        [ofs(LQ2), ofs(LQ3), ofs(LQ4), ofs(TMQ1), ofs(SCRA), ofs(SCRB),
         ofs(KRL)])]
    SG8_CH = at("SG8", [128, 2 * T], BF16, ofs(TMQ2))
    YB_CH = at("YB", [128, 2 * T], F32, ofs(A8))
    SILS = at("SILS", [128, 2 * T], BF16, ofs(MIX))
    SGB_CH = [at(f"SGB{i}", [128, 2 * T], BF16, o) for i, o in enumerate(
        [ofs(B8) + 4 * T, ofs(B8) + 8 * T, ofs(B8) + 12 * T, ofs(A8) + 8 * T,
         ofs(MIX) + 4 * T, ofs(MIX) + 8 * T, ofs(MIX) + 12 * T])]
    SG8B_CH = at("SG8B", [128, 2 * T], BF16, ofs(A8) + 12 * T)
    KRL4 = at("KRL4", [128, 4 * T], BF16, ofs(LQ1))
    p0 = trans0
    WSH = at("WSH", [128, SHW], F32, p0 + 0)
    SNDB = at("SNDB", [128, SHW], BF16, p0 + 87040)
    qo = [p0 + 130560]

    def pre_at(name, shape, dtype):
        t = at(name, shape, dtype, qo[0])
        sz = int(np.prod(shape[1:])) * (4 if dtype == F32 else 2)
        qo[0] += (sz + 31) & ~31
        return t

    QW = 352
    QWP = 2 * QW
    QAX = pre_at("QAX", [128, DT * QW], F32)
    QL1 = pre_at("QL1", [128, QWP], F32)
    QY2 = pre_at("QY2", [128, QWP], F32)
    QL2 = pre_at("QL2", [128, QWP], BF16)
    QL3 = pre_at("QL3", [128, QWP], BF16)
    QL4 = pre_at("QL4", [128, QWP], BF16)
    QM1 = pre_at("QM1", [128, QW], F32)
    QM2 = pre_at("QM2", [128, QW], F32)
    QRB = pre_at("QRB", [128, QW], F32)
    QSS = pre_at("QSS", [128, QW], BF16)
    QSA = pre_at("QSA", [128, QW], F32)
    QSG6 = pre_at("QSG6", [128, QWP], BF16)
    QSG7 = pre_at("QSG7", [128, QWP], BF16)
    QSG8 = pre_at("QSG8", [128, QWP], BF16)
    PRELUDE_TOP = qo[0]
    SG_PRE = [QL2, QL3, QL4,
              at("QSG3", [128, 1024], BF16, ofs(QM1)),
              at("QSG4", [128, 1024], BF16, ofs(QM2)),
              QSG6, QSG7]
    SG8_PRE = QSG8
    PRELUDE_TOP = p0 + 160768
    top = max(off[0], PRELUDE_TOP)
    assert top <= arena0 + 212 * 1024, f"sbuf overflow {top}"
    nc.alloc_sbuf_tensor("arena_guard", [128, top - arena0], mybir.dt.uint8)


    A_CA, A_CB, A_TFE, A_EW, A_CA0, A_CB0, A_T1, A_T2 = 0, 8, 16, 24, 32, 40, 64, 72
    H_XNH, H_YNH, H_XNH2, H_XNS, H_YNS, H_SP, H_MRH = (
        0, 8, 16, 24, 32, 40, 48)

    psA = ctx.enter_context(nc.psum_tensor("psA", [128, T], F32))
    psB = ctx.enter_context(nc.psum_tensor("psB", [128, T], F32))
    psC = ctx.enter_context(nc.psum_tensor("psC", [128, T], F32))
    psDn = ctx.enter_context(nc.psum_tensor("psDn", [128, T], F32))
    psM = ctx.enter_context(nc.psum_tensor("psM", [128, T], F32))
    psR = ctx.enter_context(nc.psum_tensor("psR", [128, T], F32))
    psS = ctx.enter_context(nc.psum_tensor("psS", [1, T], F32))
    psQ = ctx.enter_context(nc.psum_tensor("psQ", [1, T], F32))

    qv, qs = RQ("v"), RQ("s")
    qt = RQ("t", serialize=False)
    dw, dp = DmaEng("w"), DmaEng("p")
    SX = Stream("sx")
    SWB = [Stream("swb0"), Stream("swb1")]
    SPL = [Stream("spl0"), Stream("spl1")]
    SPR = Stream("spr")
    SSC = Stream("ssc")
    SST = Stream("sst")
    SFD = Stream("sfd")
    SCC = Stream("scc")
    SWT = {n: Stream("sw_" + n) for n in
           ["wk", "wv", "wr", "wo", "wfr", "wfk", "wfv", "wgt", "wut", "wdt"]}
    POOL_STREAMS = [SSC, SST, SFD, SCC] + list(SWT.values())
    ccs = SemRef()

    def tile(mega, ti, w=None):
        w = T if w is None else w
        return mega[:, ti * T: ti * T + w]

    def v3(ap, a=DT):
        return ap.rearrange("p (a t) -> p a t", a=a)

    def mmg(ps, pairs):
        n = len(pairs)
        for i, (l, r) in enumerate(pairs):
            qt.emit(lambda t, l=l, r=r, s=(i == 0), e=(i == n - 1):
                    t.matmul(ps, l, r, start=s, stop=e),
                    inc=(1 if i == n - 1 else 0))
        return qt.n

    bank_free = {}

    def bank_wait(ps):
        if ps.name in bank_free:
            q, cnt = bank_free[ps.name]
            qt.wait(q, cnt)

    def bank_evac(ps, q, cnt):
        bank_free[ps.name] = (q, cnt)

    wb_state = [(None, 0), (None, 0)]
    wb_idx = [0]
    scratch_ready_ref = [{}]

    def load_granule(dram_ap, ntile=None, wname=None):
        i = wb_idx[0] % 2
        wb_idx[0] += 1
        q, cnt = wb_state[i]
        if q is not None:
            dw.wait(q, cnt)
        if wname is not None and wname in scratch_ready_ref[0]:
            dw.wait(*scratch_ready_ref[0][wname])
        if ntile is None:
            w = dram_ap.shape[1]
            dst = WB[i][:, 0:w]
        else:
            w = dram_ap.shape[1] * dram_ap.shape[2]
            dst = WB[i][:, 0:w].rearrange("p (a t) -> p a t", a=dram_ap.shape[1])
        dw.dma(SWB[i], dst, dram_ap)
        return i, (SWB[i], SWB[i].n)

    def note_wb(i):
        wb_state[i] = (qt, qt.n)

    def barrier(streams=None):
        counts = [(qv, qv.n), (qs, qs.n), (qt, qt.n)]
        pool = [(s, s.n) for s in
                (POOL_STREAMS if streams is None else streams)]
        for q, _ in counts:
            for oq, ocnt in counts:
                if oq is not q:
                    q.wait(oq, ocnt)
            for s, cnt in pool:
                q.wait(s, cnt)
        for oq, ocnt in counts:
            dp.wait(oq, ocnt)

    def emit_ln_sums(src, deps, bs=None, bq=None, tile_deps=None):
        bs = psS if bs is None else bs
        bq = psQ if bq is None else bq
        for q, c in deps:
            qs.wait(q, c)
            qv.wait(q, c)
        if tile_deps is None:
            qs.emit(lambda s: s.activation(CSQ[:], src[:], AF.Square))
            qv.emit(lambda v: v.tensor_copy(CXB[:], src[:]))
            m_sq, m_cx = qs.n, qv.n
            qt.wait(qs, m_sq)
            qt.wait(qv, m_cx)
            bank_wait(bs)
            mmg(bs[0:1, 0:T], [(ONESC[:], tile(CXB, ti)) for ti in range(DT)])
            bank_wait(bq)
            mmg(bq[0:1, 0:T], [(ONESC[:], tile(CSQ, ti)) for ti in range(DT)])
            return {"src": src, "m_sums": qt.n, "bs": bs, "bq": bq}
        bank_wait(bs)
        bank_wait(bq)
        for ti in range(DT):
            for q, c in tile_deps[ti]:
                qs.wait(q, c)
                qv.wait(q, c)
            qs.emit(lambda s, ti=ti: s.activation(
                tile(CSQ, ti), tile(src, ti), AF.Square))
            qv.emit(lambda v, ti=ti: v.tensor_copy(tile(CXB, ti),
                                                   tile(src, ti)))
            m_sq, m_cx = qs.n, qv.n
            qt.wait(qs, m_sq)
            qt.wait(qv, m_cx)
            st, en = ti == 0, ti == DT - 1
            qt.emit(lambda t, ti=ti, st=st, en=en: t.matmul(
                bs[0:1, 0:T], ONESC[:], tile(CXB, ti), start=st, stop=en),
                inc=(1 if en else 0))
            qt.emit(lambda t, ti=ti, st=st, en=en: t.matmul(
                bq[0:1, 0:T], ONESC[:], tile(CSQ, ti), start=st, stop=en),
                inc=(1 if en else 0))
        return {"src": src, "m_sums": qt.n, "bs": bs, "bq": bq}

    def emit_ln_finish(st, dst_bf=None, dst_f32=None, gcol=None, bcol=None,
                       halo_col=None):
        src, m_sums, bs, bq = st["src"], st["m_sums"], st["bs"], st["bq"]
        MN, VAR, TA = SMALL[:, 0:T], SMALL[:, T:2 * T], SMALL[:, 2 * T:3 * T]
        qv.wait(qt, m_sums)
        qv.emit(lambda v: v.tensor_scalar(MN, bs[0:1, 0:T], 1.0 / D, None, ALU.mult))
        qv.emit(lambda v: v.tensor_scalar(TA, bq[0:1, 0:T], 1.0 / D, None, ALU.mult))
        qv.emit(lambda v: v.tensor_tensor(VAR, MN, MN, ALU.mult))
        qv.emit(lambda v: v.tensor_tensor(VAR, TA, VAR, ALU.subtract))
        m_var = qv.n
        bank_evac(bs, qv, m_var)
        bank_evac(bq, qv, m_var)
        qs.wait(qv, m_var)
        qs.emit(lambda s: s.activation(TA, VAR, AF.Ln, bias=EPS_LN))
        qs.emit(lambda s: s.activation(VAR, TA, AF.Exp, scale=-0.5))
        m_rstd = qs.n
        MNB, RSB = SMB[:, 0:T], SMB[:, T:2 * T]
        qv.wait(qs, m_rstd)
        qv.emit(lambda v: v.tensor_copy(MNB, MN))
        qv.emit(lambda v: v.tensor_copy(RSB, VAR))
        m_sb = qv.n
        qt.wait(qv, m_sb)
        bank_wait(psM)
        mmg(psM[:], [(ONESR[:], MNB)])
        bank_wait(psR)
        mmg(psR[:], [(ONESR[:], RSB)])
        m_bc = qt.n
        Mb = psM[:].unsqueeze(1).broadcast_to([128, DT, T])
        Rc = psR[:].unsqueeze(1).broadcast_to([128, DT, T])
        qv.wait(qt, m_bc)
        qv.emit(lambda v: v.tensor_tensor(v3(TMPF[:]), v3(src[:]), Mb, ALU.subtract))
        m_src_dead = qv.n
        if dst_f32 is not None:
            qv.emit(lambda v: v.tensor_tensor(v3(dst_f32[:]), v3(TMPF[:]), Rc,
                                              ALU.mult))
            m_out = qv.n
            bank_evac(psM, qv, m_out)
            bank_evac(psR, qv, m_out)
            return m_out, m_src_dead
        qv.emit(lambda v: v.tensor_tensor(v3(TMPF[:]), v3(TMPF[:]), Rc, ALU.mult))
        m = qv.n
        bank_evac(psM, qv, m)
        bank_evac(psR, qv, m)
        hv = dst_bf[:].rearrange("p (a t) -> p a t", a=DT)
        hs = HALO[:, halo_col:halo_col + DT]
        qv.emit(lambda v: v.tensor_copy(hv[:, :, 0:1], hs.unsqueeze(2)))
        for ti in range(DT):
            gc = PARS[:, gcol + ti:gcol + ti + 1]
            bc = PARS[:, bcol + ti:bcol + ti + 1]
            dsl = dst_bf[:, ti * (T + 1) + 1: ti * (T + 1) + 1 + T]
            qv.emit(lambda v, a=dsl, b=tile(TMPF, ti), g=gc, c_=bc:
                    v.tensor_scalar(a, b, g, c_, ALU.mult, ALU.add))
        return qv.n, m_src_dead

    def emit_ln(src, deps, dst_bf=None, dst_f32=None, gcol=None, bcol=None,
                halo_col=None, tile_deps=None):
        st = emit_ln_sums(src, deps, tile_deps=tile_deps)
        return emit_ln_finish(st, dst_bf=dst_bf, dst_f32=dst_f32, gcol=gcol,
                              bcol=bcol, halo_col=halo_col)

    QS_CHUNK = {"ax": AXG, "m1": TMQ1, "m2": TMQ2, "sc": TMQ2, "rb": RBq,
                "ss": SSq, "sa": SCRA, "l1": LQ1, "l2": LQ2, "l3": LQ3,
                "l4": LQ4, "pe": 2 * T, "sg": SG_CH, "sg8": SG8_CH, "y2": YB_CH,
                "sgB": SGB_CH, "sg8B": SG8B_CH,
                "_lastsg": {0: 0, 1: 0}, "_lastmc": {0: 0, 1: 0}, "_gpi": [0]}
    QS_PRE = {"ax": QAX, "m1": QM1, "m2": QM2, "sc": QM2, "rb": QRB,
              "ss": QSS, "sa": QSA, "l1": QL1, "l2": QL2, "l3": QL3,
              "l4": QL4, "pe": 1024, "sg": SG_PRE, "sg8": SG8_PRE, "y2": QY2}

    def _quant_group(gsrc, gdst, nt, W, folds, deps, qcb, S):
        AXG = S["ax"]; TMQ1 = S["m1"]; TMQ2 = S["m2"]
        SCq = S["sc"]; RBq = S["rb"]; SSq = S["ss"]; SCRA = S["sa"]
        LQ1 = S["l1"]; LQ2 = S["l2"]; LQ3 = S["l3"]; LQ4 = S["l4"]
        ax = AXG[:, 0:nt * W]
        ax3 = ax.rearrange("p (k c) -> p k c", k=nt)
        for q, c in deps:
            qs.wait(q, c)
            qv.wait(q, c)
        qs.emit(lambda s, a=ax3, b=gsrc: s.activation(a, b, AF.Abs))
        m_ax = qs.n
        qv.wait(qs, m_ax)
        M1, M2 = TMQ1[:, 0:W], TMQ2[:, 0:W]
        sls = [ax[:, k * W:(k + 1) * W] for k in range(nt)]
        if nt == 8:
            qv.emit(lambda v, m=M1, a=sls[0], b=sls[1]: v.tensor_tensor(m, a, b, ALU.max))
            qv.emit(lambda v, m=M2, a=sls[2], b=sls[3]: v.tensor_tensor(m, a, b, ALU.max))
            qv.emit(lambda v, m=M1, a=M1, b=M2: v.tensor_tensor(m, a, b, ALU.max))
            qv.emit(lambda v, m=M2, a=sls[4], b=sls[5]: v.tensor_tensor(m, a, b, ALU.max))
            qv.emit(lambda v, m=M1, a=M1, b=M2: v.tensor_tensor(m, a, b, ALU.max))
            qv.emit(lambda v, m=M2, a=sls[6], b=sls[7]: v.tensor_tensor(m, a, b, ALU.max))
            qv.emit(lambda v, m=M1, a=M1, b=M2: v.tensor_tensor(m, a, b, ALU.max))
        elif nt == 4:
            qv.emit(lambda v, m=M1, a=sls[0], b=sls[1]: v.tensor_tensor(m, a, b, ALU.max))
            qv.emit(lambda v, m=M2, a=sls[2], b=sls[3]: v.tensor_tensor(m, a, b, ALU.max))
            qv.emit(lambda v, m=M1, a=M1, b=M2: v.tensor_tensor(m, a, b, ALU.max))
        else:
            qv.emit(lambda v, m=M1, a=sls[0], b=sls[1]: v.tensor_tensor(m, a, b, ALU.max))
        lim = 128
        for f in folds:
            if f >= 32:
                qv.emit(lambda v, f=f, m1=M1, m2=M2: v.tensor_copy(
                    m2[0:f, :], m1[f:2 * f, :]))
                qv.emit(lambda v, f=f, m1=M1, m2=M2: v.tensor_tensor(
                    m1[0:f, :], m1[0:f, :], m2[0:f, :], ALU.max))
            else:
                dp.wait(qv, qv.n)
                dp.dma(SFD, M2[0:16, :], M1[16:32, :])
                qv.wait(SFD, SFD.n)
                qv.emit(lambda v, m1=M1, m2=M2: v.tensor_tensor(
                    m1[0:16, :], m1[0:16, :], m2[0:16, :], ALU.max))
            lim = f
        SCs, RBs, SSs = SCq[:, 0:W], RBq[:, 0:W], SSq[:, 0:W]
        qv.emit(lambda v, L=lim, a=SCs, m=M1: v.tensor_scalar(
            a[0:L, :], m[0:L, :], 1.0 / 6.0, 1e-12, ALU.mult, ALU.max))
        m_sc = qv.n
        qs.wait(qv, m_sc)
        qs.emit(lambda s, L=lim, a=SCs, W=W: s.activation(
            SCRA[0:L, 0:W], a[0:L, :], AF.Ln))
        qs.emit(lambda s, L=lim, a=RBs, W=W: s.activation(
            a[0:L, :], SCRA[0:L, 0:W], AF.Exp, scale=-1.0))
        m_rb = qs.n
        qv.wait(qs, m_rb)
        qv.emit(lambda v, L=lim, a=SSs, b=SCs: v.tensor_scalar(
            a[0:L, :], b[0:L, :], 0.5, None, ALU.mult))
        lvl = lim
        while lvl < 128:
            if lvl == 16:
                dp.wait(qv, qv.n)
                dp.dma(SFD, RBs[16:32, :], RBs[0:16, :])
                dp.dma(SFD, SSs[16:32, :], SSs[0:16, :])
                qv.wait(SFD, SFD.n)
            else:
                qv.emit(lambda v, L=lvl, a=RBs: v.tensor_copy(
                    a[L:2 * L, :], a[0:L, :]))
                qv.emit(lambda v, L=lvl, a=SSs: v.tensor_copy(
                    a[L:2 * L, :], a[0:L, :]))
            lvl *= 2
        piece = max(1, min(nt, S["pe"] // W))
        YSLOT = [LQ1, S["y2"]]
        pieces = list(range(0, nt, piece))
        if "sgB" in S:
            SGsets = [S["sg"], S["sgB"]]
            SG8s = [S["sg8"], S["sg8B"]]
            lastsg = S["_lastsg"]
            lastmc = S["_lastmc"]

            def emit_C(md):
                par, k, Wp, p0, r3, ssb, sgd = md
                SG = SGsets[par]
                SG8 = SG8s[par]
                sl = [t[:, 0:Wp] for t in SG]
                qv.wait(qs, sgd)
                qv.emit(lambda v, a=sl[0], b=sl[1]: v.tensor_tensor(a, a, b, ALU.add))
                qv.emit(lambda v, a=sl[2], b=sl[3]: v.tensor_tensor(a, a, b, ALU.add))
                qv.emit(lambda v, a=sl[0], b=sl[2]: v.tensor_tensor(a, a, b, ALU.add))
                qv.emit(lambda v, a=sl[4], b=sl[5]: v.tensor_tensor(a, a, b, ALU.add))
                qv.emit(lambda v, a=sl[0], b=sl[1]: v.tensor_scalar(
                    b, a, 0.5, 6.0, ALU.mult, ALU.add))
                qv.emit(lambda v, a=sl[1], b=sl[4]: v.tensor_tensor(a, a, b, ALU.add))
                qv.emit(lambda v, a=sl[6], b=sl[1]: v.scalar_tensor_tensor(
                    b, a, 2.0, b, ALU.mult, ALU.add))
                qv.emit(lambda v, a=SG8[:, 0:Wp], b=sl[4], ss=ssb, r3=r3:
                        v.tensor_tensor(r3(b), r3(a), ss, ALU.mult))
                qv.emit(lambda v, a=sl[1], b=sl[4], d=gdst, p0=p0, k=k, r3=r3:
                        v.tensor_tensor(d[:, p0:p0 + k, :], r3(a), r3(b), ALU.mult))
                lastmc[par] = qv.n

            pend = None
            for p0 in pieces:
                k = min(piece, nt - p0)
                Wp = k * W
                par = S["_gpi"][0] % 2
                S["_gpi"][0] += 1
                y = YSLOT[par][:, 0:Wp]
                r3 = lambda a, k=k: a.rearrange("p (a c) -> p a c", a=k)
                axp = ax[:, p0 * W: p0 * W + Wp]
                rbb = RBs.unsqueeze(1).broadcast_to([128, k, W])
                ssb = SSs.unsqueeze(1).broadcast_to([128, k, W])
                qv.wait(qs, lastsg[par])
                qv.emit(lambda v, y=y, a=axp, rb=rbb, r3=r3: v.tensor_tensor(
                    r3(y), r3(a), rb, ALU.mult))
                m_y = qv.n
                qs.wait(qv, m_y)
                qs.wait(qv, lastmc[par])
                SG = SGsets[par]
                SG8 = SG8s[par]
                for i, th in enumerate([0.25, 0.75, 1.25, 1.75, 2.5, 3.5, 5.0]):
                    qs.emit(lambda s, i=i, th=th, y=y, Wp=Wp, SG=SG:
                            s.activation(SG[i][:, 0:Wp], y, AF.Sign, bias=-th))
                qs.emit(lambda s, a=SG8[:, 0:Wp], b=gsrc, p0=p0, k=k, r3=r3:
                        s.activation(r3(a), b[:, p0:p0 + k, :], AF.Sign))
                lastsg[par] = qs.n
                md = (par, k, Wp, p0, r3, ssb, qs.n)
                if pend is not None:
                    emit_C(pend)
                pend = md
            emit_C(pend)
            if qcb is not None:
                qcb(qv.n)
            return
        SG = S["sg"]
        SG8 = S["sg8"]
        sg_done = {}
        for pi, p0 in enumerate(pieces):
            k = min(piece, nt - p0)
            Wp = k * W
            y = YSLOT[pi % 2][:, 0:Wp]
            r3 = lambda a, k=k: a.rearrange("p (a c) -> p a c", a=k)
            axp = ax[:, p0 * W: p0 * W + Wp]
            rbb = RBs.unsqueeze(1).broadcast_to([128, k, W])
            ssb = SSs.unsqueeze(1).broadcast_to([128, k, W])
            qv.wait(qs, sg_done.get(pi - 2, 0))
            qv.emit(lambda v, y=y, a=axp, k=k, rb=rbb, r3=r3: v.tensor_tensor(
                r3(y), r3(a), rb, ALU.mult))
            m_y = qv.n
            qs.wait(qv, m_y)
            for i, th in enumerate([0.25, 0.75, 1.25, 1.75]):
                qs.emit(lambda s, i=i, th=th, y=y, Wp=Wp: s.activation(
                    SG[i][:, 0:Wp], y, AF.Sign, bias=-th))
            qs.emit(lambda s, a=SG8[:, 0:Wp], b=gsrc, p0=p0, k=k, r3=r3:
                    s.activation(r3(a), b[:, p0:p0 + k, :], AF.Sign))
            sg_done[pi] = qs.n
            qv.emit(lambda v, y=y, l=SG[4][:, 0:Wp]: v.tensor_scalar(
                l, y, 2.5, None, ALU.is_gt))
            qv.emit(lambda v, y=y, l=SG[5][:, 0:Wp]: v.tensor_scalar(
                l, y, 3.5, None, ALU.is_gt))
            qv.emit(lambda v, y=y, l=SG[6][:, 0:Wp]: v.tensor_scalar(
                l, y, 5.0, None, ALU.is_gt))
            sl = [t[:, 0:Wp] for t in SG]
            qv.wait(qs, sg_done[pi])
            qv.emit(lambda v, a=sl[0], b=sl[1]: v.tensor_tensor(a, a, b, ALU.add))
            qv.emit(lambda v, a=sl[2], b=sl[3]: v.tensor_tensor(a, a, b, ALU.add))
            qv.emit(lambda v, a=sl[0], b=sl[2]: v.tensor_tensor(a, a, b, ALU.add))
            qv.emit(lambda v, a=sl[4], b=sl[5]: v.tensor_tensor(a, a, b, ALU.add))
            qv.emit(lambda v, a=sl[0], b=sl[1]: v.tensor_scalar(
                b, a, 0.5, 2.0, ALU.mult, ALU.add))
            qv.emit(lambda v, a=sl[4], b=sl[1]: v.scalar_tensor_tensor(
                b, a, 2.0, b, ALU.mult, ALU.add))
            qv.emit(lambda v, a=sl[6], b=sl[1]: v.scalar_tensor_tensor(
                b, a, 4.0, b, ALU.mult, ALU.add))
            qv.emit(lambda v, a=SG8[:, 0:Wp], b=sl[4], k=k, ss=ssb, r3=r3:
                    v.tensor_tensor(r3(b), r3(a), ss, ALU.mult))
            qv.emit(lambda v, a=sl[1], b=sl[4], d=gdst, p0=p0, k=k, r3=r3:
                    v.tensor_tensor(d[:, p0:p0 + k, :], r3(a), r3(b), ALU.mult))
        if qcb is not None:
            qcb(qv.n)

    def emit_quant(src, src_stride, c0, W, groups, deps, dst, dst_stride,
                   dst_c0, dst_tile_base=None, group_cb=None, S=None):
        S = S if S is not None else QS_CHUNK
        for (t0, nt, folds) in groups:
            src_ap = src if isinstance(src, bass.AP) else src[:]
            dst_ap = dst if isinstance(dst, bass.AP) else dst[:]
            gsrc = src_ap.rearrange("p (k c) -> p k c", c=src_stride)[
                :, t0:t0 + nt, c0:c0 + W]
            dt0 = t0 if dst_tile_base is None else dst_tile_base
            gdst = dst_ap.rearrange("p (k c) -> p k c", c=dst_stride)[
                :, dt0:dt0 + nt, dst_c0:dst_c0 + W]
            qcb = None
            if group_cb is not None:
                qcb = (lambda m_v, t0=t0, nt=nt: group_cb(t0, nt, m_v))
            _quant_group(gsrc, gdst, nt, W, folds, deps, qcb, S)
        return qv.n, qs.n

    def emit_sigmoid(src_ap, dst_ap, w, deps=()):
        for q, c in deps:
            qs.wait(q, c)
        qs.emit(lambda s: s.activation(SCRA[:, 0:w], src_ap, AF.Exp, scale=-1.0))
        qs.emit(lambda s: s.activation(SCRB[:, 0:w], SCRA[:, 0:w], AF.Ln, bias=1.0))
        qs.emit(lambda s: s.activation(dst_ap, SCRB[:, 0:w], AF.Exp, scale=-1.0))
        return qs.n

    def emit_tiny_ln(src_off, dst_halo, gcol, bcol, gate_col):
        src = AUX[:, src_off:src_off + DT]
        qv.emit(lambda v: v.tensor_copy(HALO[:, H_SP:H_SP + DT], src))
        m_cast = qv.n
        qs.wait(qv, m_cast)
        qs.emit(lambda s: s.activation(KRL[:, 0:DT], src, AF.Square))
        m_sq = qs.n
        qt.wait(qv, m_cast)
        qt.wait(qs, m_sq)
        bank_wait(psS)
        mmg(psS[:, 0:DT], [(ONESC[:], HALO[:, H_SP:H_SP + DT])])
        bank_wait(psQ)
        mmg(psQ[:, 0:DT], [(ONESC[:], KRL[:, 0:DT])])
        m_mm = qt.n
        s1 = SMALL[:, 3 * T:3 * T + 1]
        s2 = SMALL[:, 3 * T + 1:3 * T + 2]
        s3 = SMALL[:, 3 * T + 2:3 * T + 3]
        qv.wait(qt, m_mm)
        qv.emit(lambda v: v.reduce_sum(s1, psS[:, 0:DT], axis=mybir.AxisListType.X))
        qv.emit(lambda v: v.reduce_sum(s2, psQ[:, 0:DT], axis=mybir.AxisListType.X))
        qv.emit(lambda v: v.tensor_scalar(s1, s1, 1.0 / D, None, ALU.mult))
        qv.emit(lambda v: v.tensor_scalar(s2, s2, 1.0 / D, None, ALU.mult))
        qv.emit(lambda v: v.tensor_tensor(s3, s1, s1, ALU.mult))
        qv.emit(lambda v: v.tensor_tensor(s2, s2, s3, ALU.subtract))
        m_var = qv.n
        bank_evac(psS, qv, m_var)
        bank_evac(psQ, qv, m_var)
        qs.wait(qv, m_var)
        qs.emit(lambda s: s.activation(s3, s2, AF.Ln, bias=EPS_LN))
        qs.emit(lambda s: s.activation(s2, s3, AF.Exp, scale=-0.5))
        m_rstd = qs.n
        qv.wait(qs, m_rstd)
        qv.emit(lambda v: v.tensor_copy(SMB[:, 0:1], s1))
        qv.emit(lambda v: v.tensor_copy(SMB[:, 1:2], s2))
        m_sb = qv.n
        qt.wait(qv, m_sb)
        bank_wait(psM)
        mmg(psM[:, 0:2], [(ONESR[:], SMB[:, 0:2])])
        m_bc = qt.n
        qv.wait(qt, m_bc)
        mb = psM[:, 0:1].broadcast_to([128, DT])
        rb = psM[:, 1:2].broadcast_to([128, DT])
        t1 = AUX[:, A_T1:A_T1 + DT]
        qv.emit(lambda v: v.tensor_tensor(t1, src, mb, ALU.subtract))
        qv.emit(lambda v: v.tensor_tensor(t1, t1, rb, ALU.mult))
        m = qv.n
        bank_evac(psM, qv, m)
        if gcol is not None:
            for ti in range(DT):
                qv.emit(lambda v, ti=ti: v.tensor_scalar(
                    t1[:, ti:ti + 1], t1[:, ti:ti + 1],
                    PARS[:, gcol + ti:gcol + ti + 1],
                    PARS[:, bcol + ti:bcol + ti + 1], ALU.mult, ALU.add))
        if gate_col is not None:
            qv.emit(lambda v: v.tensor_tensor(
                t1, t1, PARS[:, gate_col:gate_col + DT], ALU.mult))
        qv.emit(lambda v: v.tensor_copy(HALO[:, dst_halo:dst_halo + DT], t1))
        return qv.n


    dw.dma(SPR, PARS[:], par_d[:, :])
    dw.dma(SPR, GT0[:], gt0_d[:, :])
    m_par = SPR.n
    qv.wait(SPR, m_par)
    qv.emit(lambda v: v.memset(ONESC[:], 1.0))
    qv.emit(lambda v: v.memset(ONESR[:], 1.0))
    qv.emit(lambda v: v.memset(AUX[:], 0.0))
    qv.emit(lambda v: v.memset(HALO[:], 0.0))
    for _ci, _cv in enumerate(CONST_VALS):
        qv.emit(lambda v, ci=_ci, cv=_cv: v.memset(CONSTS[:, ci:ci + 1], cv))
    m_consts = qv.n
    qs.wait(SPR, m_par)
    qs.wait(qv, m_consts)
    qs.emit(lambda s: s.activation(SCRA[:, 0:DT],
                                   PARS[:, PAR["tdecay"]:PAR["tdecay"] + DT],
                                   AF.Exp))
    qs.emit(lambda s: s.activation(AUX[:, A_EW:A_EW + DT], SCRA[:, 0:DT],
                                   AF.Exp, scale=-1.0))
    m_ew = qs.n
    qv.wait(qs, m_ew)
    qv.emit(lambda v: v.tensor_tensor(
        AUX[:, A_TFE:A_TFE + DT], PARS[:, PAR["tfirst"]:PAR["tfirst"] + DT],
        PARS[:, PAR["flfirst"]:PAR["flfirst"] + DT], ALU.mult))
    qv.emit(lambda v: v.tensor_copy(AUX[:, A_T2:A_T2 + DT],
                                    PARS[:, PAR["xprev"]:PAR["xprev"] + DT]))
    emit_tiny_ln(A_T2, H_XNH, PAR["ln1g"], PAR["ln1b"], PAR["flcarry"])
    qv.emit(lambda v: v.tensor_copy(AUX[:, A_T2:A_T2 + DT],
                                    PARS[:, PAR["xprev2"]:PAR["xprev2"] + DT]))
    emit_tiny_ln(A_T2, H_XNH2, PAR["ln1g"], PAR["ln1b"], PAR["flcarry"])

    dw.wait(qt, qt.n)
    dw.wait(qs, qs.n)
    dw.wait(qv, qv.n)
    dw.dma(SPL[0], WSH[:, 0:AG1W], wsh[:, 0:AG1W])
    m_wsh1 = SPL[0].n
    dw.dma(SPL[1], WSH[:, AG1W:SHW], wsh[:, AG1W:SHW])
    m_wsh = SPL[1].n
    for name, bpc, cols in SH_PLAIN:
        o = SH_OFF[name]
        w = bpc * cols
        if o + w <= AG1W:
            qs.wait(SPL[0], m_wsh1)
        else:
            qs.wait(SPL[1], m_wsh)
        sc = 0.5 if name == "wfk" else None
        if sc is None:
            qs.emit(lambda s, o=o, w=w: s.activation(
                SNDB[:, o:o + w], WSH[:, o:o + w], AF.Copy))
        else:
            qs.emit(lambda s, o=o, w=w, sc=sc: s.activation(
                SNDB[:, o:o + w], WSH[:, o:o + w], AF.Copy, scale=sc))
        if name == "wv":
            dp.wait(qs, qs.n)
            dp.dma(SSC, snd_w1.ap(), SNDB[:, 0:AG1W])
            dp.wait(SSC, SSC.n)
            dp.raw(lambda g: g.collective_compute(
                "AllGather", ALU.bypass,
                replica_groups=[[0, 1, 2, 3, 4, 5, 6, 7]],
                ins=[snd_w1.ap().opt()], outs=[rcv_w1.ap().opt()]), sem=ccs)
    for name, kt, cols, shw in SH_Q:
        o = SH_OFF[name]
        groups = GROUPS_D if kt == DT else GROUPS_H
        emit_quant(WSH[:, o:o + kt * shw], shw, 0, shw, groups,
                   [(SPL[1], m_wsh)],
                   SNDB[:, o:o + kt * shw], shw, 0, S=QS_PRE)
    dp.wait(qs, qs.n)
    dp.wait(qv, qv.n)
    dp.dma(SSC, snd_w2.ap(), SNDB[:, AG1W:SHW])
    m_sndw = SSC.n
    dp.wait(SSC, SSC.n)
    dp.raw(lambda g: g.collective_compute(
        "AllGather", ALU.bypass,
        replica_groups=[[0, 1, 2, 3, 4, 5, 6, 7]],
        ins=[snd_w2.ap().opt()], outs=[rcv_w2.ap().opt()]), sem=ccs)
    scratch_ready = {}
    rcv1 = rcv_w1.ap()
    rcv3 = rcv_w2.ap()
    dp.wait(ccs, 1)
    for name, bpc, cols in SH_PLAIN:
        o = SH_OFF[name]
        w = bpc * cols
        if o + w > AG1W and name == "wr":
            dp.wait(ccs, 2)
        for r in range(8):
            src = (rcv1[r * 128:(r + 1) * 128, o:o + w] if o + w <= AG1W
                   else rcv3[r * 128:(r + 1) * 128, o - AG1W:o - AG1W + w])
            dp.dma(SWT[name], wsc[name][:, r * w:(r + 1) * w], src)
        scratch_ready[name] = (SWT[name], SWT[name].n)
    for name, kt, cols, shw in SH_Q:
        o = SH_OFF[name] - AG1W
        for r in range(8):
            dst3 = wsc[name].rearrange("p (k c) -> p k c", c=cols)[
                :, :, r * shw:(r + 1) * shw]
            src3 = rcv3[r * 128:(r + 1) * 128, o:o + kt * shw].rearrange(
                "p (k c) -> p k c", c=shw)
            dp.dma(SWT[name], dst3, src3)
        scratch_ready[name] = (SWT[name], SWT[name].n)
    scratch_ready_ref[0] = scratch_ready

    for q in (qv, qs, qt):
        for oq, ocnt in [(qv, qv.n), (qs, qs.n), (qt, qt.n)]:
            if oq is not q:
                q.wait(oq, ocnt)
        q.wait(SSC, m_sndw)
    dw.wait(qv, qv.n)
    dw.wait(qs, qs.n)
    dw.wait(SSC, m_sndw)

    EWc = lambda ti: AUX[:, A_EW + ti:A_EW + ti + 1]

    state = {"xload": 0, "store": 0, "m_fin_dead": 0}

    def x_load(c, tile_waits=None):
        for d in range(DT):
            if tile_waits is not None:
                dw.wait(qv, tile_waits[d])
            dw.dma(SX, X[:, d * T:(d + 1) * T],
                   xT[d * 128:(d + 1) * 128, c * T:(c + 1) * T])
        state["xload"] = SX.n

    def chunk_common_attn(c, pass2, first_chunk_carry0, early=None,
                          ln1_deps=()):
        T1 = T + 1
        if early is None:
            early = emit_ln_sums(X, [(SX, state["xload"])] + list(ln1_deps))
        m_ln1, _ = emit_ln_finish(early, dst_bf=XN,
                                  gcol=PAR["ln1g"], bcol=PAR["ln1b"],
                                  halo_col=(H_XNH if c == 0 else H_XNS))
        hvx = XN[:].rearrange("p (a t) -> p a t", a=DT)
        qv.emit(lambda v: v.tensor_copy(HALO[:, H_XNS:H_XNS + DT].unsqueeze(2),
                                        hvx[:, :, T1 - 1:T1]))
        cur = XN[:].rearrange("p (a t) -> p a t", a=DT)[:, :, 1:T1]
        prv = XN[:].rearrange("p (a t) -> p a t", a=DT)[:, :, 0:T]
        qv.emit(lambda v: v.tensor_tensor(v3(DDb[:]), cur, prv, ALU.subtract))
        m_dd = qv.n

        def mix_into(dstmega, mcol):
            for ti in range(DT):
                qv.emit(lambda v, ti=ti, d=dstmega, mc=mcol:
                        v.scalar_tensor_tensor(
                            tile(d, ti), tile(DDb, ti),
                            PARS[:, mc + ti:mc + ti + 1],
                            XN[:, ti * T1:ti * T1 + T], ALU.mult, ALU.add))
            return qv.n

        m_mk = mix_into(MIX, PAR["mixk"])
        bk, m_wk = load_granule(wsc["wk"], wname="wk")
        for d in range(DT):
            ps = psA if d % 2 == 0 else psB
            bank_wait(ps)
            qt.wait(qv, m_mk)
            qt.wait(*m_wk)
            mmg(ps[:], [(WB[bk][:, kk_ * D + d * 128: kk_ * D + (d + 1) * 128],
                         tile(MIX, kk_)) for kk_ in range(DT)])
            m_g = qt.n
            qs.wait(qt, m_g)
            if c == 0:
                qs.emit(lambda s, d=d, ps=ps: s.activation(
                    EK[:, d * T:d * T + 1], ps[:, 0:1], AF.Exp,
                    bias=AUX[:, A_TFE + d:A_TFE + d + 1]))
                qs.emit(lambda s, d=d, ps=ps: s.activation(
                    EK[:, d * T + 1:(d + 1) * T], ps[:, 1:T], AF.Exp))
            else:
                qs.emit(lambda s, d=d, ps=ps: s.activation(
                    tile(EK, d), ps[:], AF.Exp))
            bank_evac(ps, qs, qs.n)
        note_wb(bk)
        m_kmm = qt.n
        m_kepi = qs.n
        m_mv = mix_into(MIX2, PAR["mixv"])
        if pass2:
            qv.wait(qt, m_kmm)
            m_mr = mix_into(MIX, PAR["mixr"])
        bv, m_wv = load_granule(wsc["wv"], wname="wv")
        m_vept = [0] * DT
        for d in range(DT):
            ps = psA if d % 2 == 0 else psB
            bank_wait(ps)
            qt.wait(qv, m_mv)
            qt.wait(*m_wv)
            mmg(ps[:], [(WB[bv][:, kk_ * D + d * 128: kk_ * D + (d + 1) * 128],
                         tile(MIX2, kk_)) for kk_ in range(DT)])
            m_g = qt.n
            qs.wait(qt, m_g)
            qs.emit(lambda s, d=d, ps=ps: s.activation(
                tile(VV, d), ps[:], AF.Copy))
            m_vept[d] = qs.n
            bank_evac(ps, qs, qs.n)
        note_wb(bv)
        m_vepi = qs.n
        m_r_epi = None
        if pass2:
            m_mv = m_mr
            br, m_wr = load_granule(wsc["wr"], wname="wr")
            for d in range(DT):
                ps = psA if d % 2 == 0 else psB
                bank_wait(ps)
                qt.wait(qv, m_mr)
                qt.wait(*m_wr)
                mmg(ps[:], [(WB[br][:, kk_ * D + d * 128: kk_ * D + (d + 1) * 128],
                             tile(MIX, kk_)) for kk_ in range(DT)])
                m_g = qt.n
                emit_sigmoid(ps[:], tile(Rb, d), T, deps=[(qt, m_g)])
                bank_evac(ps, qs, qs.n)
            note_wb(br)
            m_r_epi = qs.n
        m_scant = [0] * DT
        for ti in range(DT):
            if first_chunk_carry0 and c == 0:
                init_a = 0.0
                init_b = 0.0
            elif c == 0:
                init_a = AUX[:, A_CA0 + ti:A_CA0 + ti + 1]
                init_b = AUX[:, A_CB0 + ti:A_CB0 + ti + 1]
            else:
                init_a = AUX[:, A_CA + ti:A_CA + ti + 1]
                init_b = AUX[:, A_CB + ti:A_CB + ti + 1]
            qv.wait(qs, m_vept[ti])
            qv.emit(lambda v, ti=ti: v.tensor_tensor(
                tile(UU, ti), tile(EK, ti), tile(VV, ti), ALU.mult))
            qv.emit(lambda v, ti=ti, ia=init_a: v.tensor_tensor_scan(
                tile(A8, ti), EWc(ti).broadcast_to([128, T]), tile(UU, ti),
                ia, ALU.mult, ALU.add))
            qv.emit(lambda v, ti=ti, ib=init_b: v.tensor_tensor_scan(
                tile(B8, ti), EWc(ti).broadcast_to([128, T]), tile(EK, ti),
                ib, ALU.mult, ALU.add))
            m_scant[ti] = qv.n
        av = A8[:].rearrange("p (a t) -> p a t", a=DT)
        bv8 = B8[:].rearrange("p (a t) -> p a t", a=DT)
        qv.emit(lambda v: v.tensor_copy(AUX[:, A_CA:A_CA + DT].unsqueeze(2),
                                        av[:, :, T - 1:T]))
        qv.emit(lambda v: v.tensor_copy(AUX[:, A_CB:A_CB + DT].unsqueeze(2),
                                        bv8[:, :, T - 1:T]))
        m_scan = qv.n
        return m_ln1, (m_scan, m_scant), m_r_epi, m_mv

    x_load(0)
    barrier(streams=[SSC, SST, SFD, SCC])
    prev_mv = 0
    for c in range(NCH):
        m_ln1, m_scan, _, m_mv = chunk_common_attn(
            c, pass2=False, first_chunk_carry0=True,
            ln1_deps=[(qv, prev_mv)])
        prev_mv = m_mv
        if c + 1 < NCH:
            dw.wait(qv, m_ln1)
            x_load(c + 1)

    dp.wait(qv, qv.n)
    dp.dma(SCC, snd.ap(), AUX[:, 0:16])
    dp.wait(SCC, SCC.n)
    dp.raw(lambda g: g.collective_compute(
        "AllGather", ALU.bypass,
        replica_groups=[[0, 1], [2, 3], [4, 5], [6, 7]],
        ins=[snd.ap().opt()], outs=[rcv.ap().opt()]), sem=ccs)
    dp.wait(ccs, 3)
    dp.dma(SCC, AUXG[:], rcv.ap()[0:128, :])
    m_ccd = SCC.n
    qv.wait(SCC, m_ccd)
    qv.emit(lambda v: v.tensor_tensor(AUX[:, A_CA0:A_CA0 + 16], AUXG[:],
                                      GT0[:], ALU.mult))
    m_cu = qv.n

    qs.wait(qv, m_cu)
    qs.emit(lambda s: s.activation(SCRA[:, 0:DT], AUX[:, A_CB0:A_CB0 + DT],
                                   AF.Ln, bias=EPS_WKV))
    qs.emit(lambda s: s.activation(SCRA[:, 8:8 + DT], SCRA[:, 0:DT],
                                   AF.Exp, scale=-1.0))
    m_rb = qs.n
    qv.wait(qs, m_rb)
    qv.emit(lambda v: v.tensor_tensor(AUX[:, A_T2:A_T2 + DT],
                                      AUX[:, A_CA0:A_CA0 + DT],
                                      SCRA[:, 8:8 + DT], ALU.mult))
    qv.emit(lambda v: v.tensor_tensor(HALO[:, H_SP:H_SP + DT],
                                      HALO[:, H_XNH:H_XNH + DT],
                                      HALO[:, H_XNH2:H_XNH2 + DT],
                                      ALU.subtract))
    for ti in range(DT):
        qv.emit(lambda v, ti=ti: v.scalar_tensor_tensor(
            HALO[:, H_MRH + ti:H_MRH + ti + 1],
            HALO[:, H_SP + ti:H_SP + ti + 1],
            PARS[:, PAR["mixr"] + ti:PAR["mixr"] + ti + 1],
            HALO[:, H_XNH2 + ti:H_XNH2 + ti + 1], ALU.mult, ALU.add))
    m_mrh = qv.n
    br, m_wr = load_granule(wsc["wr"], wname="wr")
    bank_wait(psA)
    for d in range(DT):
        qt.wait(qv, m_mrh)
        qt.wait(*m_wr)
        mmg(psA[:, d:d + 1],
            [(WB[br][:, kk_ * D + d * 128: kk_ * D + (d + 1) * 128],
              HALO[:, H_MRH + kk_:H_MRH + kk_ + 1]) for kk_ in range(DT)])
    note_wb(br)
    m_rh_mm = qt.n
    emit_sigmoid(psA[:, 0:DT], SCRB[:, 0:DT], DT, deps=[(qt, m_rh_mm)])
    m_rh = qs.n
    bank_evac(psA, qs, m_rh)
    qv.wait(qs, m_rh)
    qv.emit(lambda v: v.tensor_tensor(HALO[:, H_SP:H_SP + DT],
                                      SCRB[:, 0:DT], AUX[:, A_T2:A_T2 + DT],
                                      ALU.mult))
    m_rwh = qv.n
    bo, m_wo = load_granule(wsc["wo"], wname="wo")
    bank_wait(psB)
    for d in range(DT):
        qt.wait(qv, m_rwh)
        qt.wait(*m_wo)
        mmg(psB[:, d:d + 1],
            [(WB[bo][:, kk_ * D + d * 128: kk_ * D + (d + 1) * 128],
              HALO[:, H_SP + kk_:H_SP + kk_ + 1]) for kk_ in range(DT)])
    note_wb(bo)
    m_oh_mm = qt.n
    qv.wait(qt, m_oh_mm)
    qv.emit(lambda v: v.tensor_tensor(AUX[:, A_T2:A_T2 + DT],
                                      PARS[:, PAR["xprev"]:PAR["xprev"] + DT],
                                      psB[:, 0:DT], ALU.add))
    bank_evac(psB, qv, qv.n)
    emit_tiny_ln(A_T2, H_YNH, PAR["ln2g"], PAR["ln2b"], PAR["flcarry"])

    dw.wait(qv, qv.n)
    x_load(0)

    barrier()
    early_next = [None]
    for c in range(NCH):
        qv.wait(SST, state["store"])
        qs.wait(SST, state["store"])
        TAG[0] = f"p2c{c}"
        m_ln1, (m_scan, m_scant), m_r_epi, _ = chunk_common_attn(
            c, pass2=True, first_chunk_carry0=False, early=early_next[0])
        for ti in range(DT):
            qs.wait(qv, m_scant[ti])
            qs.emit(lambda s, ti=ti: s.activation(
                SCRA[:], tile(B8, ti), AF.Ln, bias=EPS_WKV))
            qs.emit(lambda s, ti=ti: s.activation(
                tile(RBW, ti), SCRA[:], AF.Exp, scale=-1.0))
        m_rbw = qs.n
        qv.wait(qs, m_rbw)
        qv.emit(lambda v: v.tensor_tensor(WKV[:], A8[:], RBW[:], ALU.mult))
        qv.wait(qs, m_r_epi)
        qv.emit(lambda v: v.tensor_tensor(RWKV[:], WKV[:], Rb[:], ALU.mult))
        m_rwkv = qv.n
        bo, m_wo = load_granule(wsc["wo"], wname="wo")
        m_x1t = [0] * DT
        for d in range(DT):
            ps = psA if d % 2 == 0 else psB
            bank_wait(ps)
            qt.wait(qv, m_rwkv)
            qt.wait(*m_wo)
            mmg(ps[:], [(WB[bo][:, kk_ * D + d * 128: kk_ * D + (d + 1) * 128],
                         tile(RWKV, kk_)) for kk_ in range(DT)])
            m_g = qt.n
            qv.wait(qt, m_g)
            qv.emit(lambda v, d=d, ps=ps: v.tensor_tensor(
                tile(X1, d), tile(X, d), ps[:], ALU.add))
            m_x1t[d] = qv.n
            bank_evac(ps, qv, qv.n)
        note_wb(bo)
        m_x1 = qv.n
        TAG[0] = TAG[0].split("-")[0] + "-ln2"
        m_ln2, _ = emit_ln(X1, [], dst_bf=XN,
                           gcol=PAR["ln2g"], bcol=PAR["ln2b"],
                           halo_col=(H_YNH if c == 0 else H_YNS),
                           tile_deps=[[(qv, m_x1t[d])] for d in range(DT)])
        T1 = T + 1
        hvy = XN[:].rearrange("p (a t) -> p a t", a=DT)
        qv.emit(lambda v: v.tensor_copy(HALO[:, H_YNS:H_YNS + DT].unsqueeze(2),
                                        hvy[:, :, T1 - 1:T1]))
        cur = hvy[:, :, 1:T1]
        prv = hvy[:, :, 0:T]
        qv.emit(lambda v: v.tensor_tensor(v3(YMIX[:]), cur, prv, ALU.add))
        m_ym = qv.n
        sq_done = {}
        for s_ in range(4):
            bf, m_wf = load_granule(
                wsc["wfk"].rearrange("p (k c) -> p k c", c=KD)[
                    :, :, s_ * 1024:(s_ + 1) * 1024], ntile=DT, wname="wfk")
            for dl in range(8):
                dg = s_ * 8 + dl
                ps = psA if dg % 2 == 0 else psB
                bank_wait(ps)
                qt.wait(qv, m_ym)
                qt.wait(*m_wf)
                mmg(ps[:], [(WB[bf][:, kk_ * 1024 + dl * 128:
                                   kk_ * 1024 + (dl + 1) * 128],
                             tile(YMIX, kk_)) for kk_ in range(DT)])
                m_g = qt.n
                krl = KRL4[:, (dg % 4) * T:(dg % 4) * T + T]
                qs.wait(qt, m_g)
                qs.wait(qv, sq_done.get(dg - 4, 0))
                qs.emit(lambda s, ps=ps, k=krl: s.activation(k, ps[:], AF.Relu))
                m_rl = qs.n
                bank_evac(ps, qs, qs.n)
                qv.wait(qs, m_rl)
                qv.emit(lambda v, k=krl, dg=dg: v.tensor_tensor(
                    tile(KK, dg), k, k, ALU.mult))
                sq_done[dg] = qv.n
            note_wb(bf)
        m_kk = (qv, qv.n)
        ync = [XN[:, ti * T1 + 1: ti * T1 + 1 + T] for ti in range(DT)]
        bfr, m_wfr = load_granule(wsc["wfr"], wname="wfr")
        for d in range(DT):
            ps = psC if d % 2 == 0 else psDn
            bank_wait(ps)
            qt.wait(qv, m_ln2)
            qt.wait(*m_wfr)
            mmg(ps[:], [(WB[bfr][:, kk_ * D + d * 128: kk_ * D + (d + 1) * 128],
                         ync[kk_]) for kk_ in range(DT)])
            m_g = qt.n
            emit_sigmoid(ps[:], tile(RRb, d), T, deps=[(qt, m_g), (qv, m_rwkv)])
            bank_evac(ps, qs, qs.n)
        note_wb(bfr)
        m_rr = qs.n
        m_x3t = [0] * DT
        for s_ in range(4):
            bfv, m_wv2 = load_granule(
                wsc["wfv"].rearrange("p (k c) -> p k c", c=D)[
                    :, :, s_ * 256:(s_ + 1) * 256], ntile=KT, wname="wfv")
            for dl in range(2):
                d = s_ * 2 + dl
                ps = psA if d % 2 == 0 else psB
                bank_wait(ps)
                qt.wait(*m_kk)
                qt.wait(*m_wv2)
                mmg(ps[:], [(WB[bfv][:, kk_ * 256 + dl * 128:
                                    kk_ * 256 + (dl + 1) * 128],
                             tile(KK, kk_)) for kk_ in range(KT)])
                m_g = qt.n
                qv.wait(qt, m_g)
                qv.wait(qs, m_rr)
                qv.emit(lambda v, d=d, ps=ps: v.tensor_tensor(
                    SCRA[:], tile(RRb, d), ps[:], ALU.mult))
                qv.emit(lambda v, d=d: v.tensor_tensor(
                    SCRB[:], SCRA[:], tile(X1, d), ALU.add))
                qv.emit(lambda v, d=d: v.tensor_tensor(
                    tile(X3, d), SCRB[:], tile(X, d), ALU.add))
                m_x3t[d] = qv.n
                bank_evac(ps, qv, qv.n)
            note_wb(bfv)
        m_s = qv.n
        if c + 1 < NCH:
            x_load(c + 1, tile_waits=m_x3t)
        TAG[0] = f"p2c{c}-postln"
        m_x3, _ = emit_ln(X3, [(qv, m_s)], dst_f32=X3)
        if c + 1 < NCH:
            TAG[0] = f"p2c{c}-early"
            early_next[0] = emit_ln_sums(X, [(SX, state["xload"])],
                                         bs=psC, bq=psDn)
        else:
            early_next[0] = None
        m_xq, m_xq_s = emit_quant(X3, T, 0, T, GROUPS_D,
                                  [(qv, m_x3)], XQ, T, 0)
        silu_last = [0, 0]
        slabs = [(0, 8), (8, 8), (16, 6)]
        for j0, jw in slabs:
            bg, m_wg = load_granule(
                wsc["wgt"].rearrange("p (k c) -> p k c", c=HD)[
                    :, :, j0 * 128:(j0 + jw) * 128], ntile=DT, wname="wgt")
            bu, m_wu = load_granule(
                wsc["wut"].rearrange("p (k c) -> p k c", c=HD)[
                    :, :, j0 * 128:(j0 + jw) * 128], ntile=DT, wname="wut")
            for jl in range(jw):
                j = j0 + jl
                psG = psA if j % 2 == 0 else psM
                psU = psB if j % 2 == 0 else psR
                bank_wait(psG)
                qt.wait(qv, m_xq)
                qt.wait(qs, m_xq_s)
                qt.wait(*m_wg)
                mmg(psG[:], [(WB[bg][:, kk_ * jw * 128 + jl * 128:
                                    kk_ * jw * 128 + (jl + 1) * 128],
                              tile(XQ, kk_)) for kk_ in range(DT)])
                bank_wait(psU)
                qt.wait(*m_wu)
                mmg(psU[:], [(WB[bu][:, kk_ * jw * 128 + jl * 128:
                                    kk_ * jw * 128 + (jl + 1) * 128],
                              tile(XQ, kk_)) for kk_ in range(DT)])
                m_g = qt.n
                sil = SILS[:, (j % 2) * T:(j % 2) * T + T]
                qs.wait(qt, m_g)
                qs.wait(qv, silu_last[j % 2])
                qs.emit(lambda s, ps=psG, a=sil: s.activation(a, ps[:], AF.Silu))
                m_sig = qs.n
                bank_evac(psG, qs, m_sig)
                qv.wait(qs, m_sig)
                qv.emit(lambda v, ps=psU, j=j, a=sil: v.tensor_tensor(
                    tile(HH, j), a, ps[:], ALU.mult))
                bank_evac(psU, qv, qv.n)
                silu_last[j % 2] = qv.n
                m_after = qv.n
                for (t0g, ntg, foldsg) in GROUPS_H:
                    if t0g + ntg - 1 == j:
                        emit_quant(HH, T, 0, T, [(t0g, ntg, foldsg)],
                                   [(qv, m_after)], HQ, T, 0)
            note_wb(bg)
            note_wb(bu)
        m_hq, m_hq_s = qv.n, qs.n
        m_ft = [0] * DT
        for s_ in range(4):
            bd, m_wd = load_granule(
                wsc["wdt"].rearrange("p (k c) -> p k c", c=D)[
                    :, :, s_ * 256:(s_ + 1) * 256], ntile=HT, wname="wdt")
            for dl in range(2):
                d = s_ * 2 + dl
                ps = psA if d % 2 == 0 else psB
                bank_wait(ps)
                qt.wait(qv, m_hq)
                qt.wait(qs, m_hq_s)
                qt.wait(*m_wd)
                mmg(ps[:], [(WB[bd][:, kk_ * 256 + dl * 128:
                                   kk_ * 256 + (dl + 1) * 128],
                             tile(HQ, kk_)) for kk_ in range(HT)])
                m_g = qt.n
                qv.wait(qt, m_g)
                qv.emit(lambda v, d=d, ps=ps: v.tensor_tensor(
                    tile(Fs, d), tile(X3, d), ps[:], ALU.add))
                m_ft[d] = qv.n
                bank_evac(ps, qv, qv.n)
            note_wb(bd)
        m_f = qv.n
        fdeps = []
        if early_next[0] is not None:
            fdeps.append((qt, early_next[0]["m_sums"]))
        TAG[0] = f"p2c{c}-final"
        m_out, m_f_dead = emit_ln(Fs, fdeps, dst_f32=OUTb,
                                  tile_deps=[[(qv, m_ft[d])]
                                             for d in range(DT)])
        dp.wait(qv, m_out)
        dp.dma(SST, outT.rearrange("(a p) n -> p a n", p=128)[:, :, c * T:(c + 1) * T],
               OUTb[:].rearrange("p (a t) -> p a t", a=DT))
        state["store"] = SST.n

    for s in POOL_STREAMS:
        dp.wait(s, s.n)
    dp.raw(lambda g: g.engine_nop())

    RQNC[0] = nc
    with ExitStack() as bctx:
        for q in (qv, qs, qt):
            q.sem = bctx.enter_context(nc.semaphore("sem_" + q.name))
        for s in [SX, SPR, SSC, SST, SFD, SCC] + SWB + SPL + list(SWT.values()):
            s.sem = bctx.enter_context(nc.semaphore("sem_" + s.name))
        ccs.sem = bctx.enter_context(nc.semaphore("sem_cc"))
        block = bctx.enter_context(nc.Block())

        @block.sync
        def _(eng):
            dw.replay(eng)

        @block.gpsimd
        def _(eng):
            dp.replay(eng)

        @block.vector
        def _(eng):
            qv.replay(eng)

        @block.scalar
        def _(eng):
            qs.replay(eng)

        @block.tensor
        def _(eng):
            qt.replay(eng)

    return nc


_RUN = [run_bass_kernel_spmd]


def kernel(L0_attn_x, L0_attn_ln1_g, L0_attn_ln1_b, L0_attn_Wk, L0_attn_Wv,
           L0_attn_Wr, L0_attn_Wo, L0_attn_time_decay, L0_attn_time_first,
           L0_attn_mix_k, L0_attn_mix_v, L0_attn_mix_r, L0_attn_ln2_g,
           L0_attn_ln2_b, L0_attn_Wfk, L0_attn_Wfv, L0_attn_Wfr,
           L0_ffn_w_gate, L0_ffn_w_up, L0_ffn_w_down, _tokens=None, _T=None):
    x = np.asarray(L0_attn_x, np.float32)
    B, L, _ = x.shape
    tokens = _tokens if _tokens is not None else (B * L) // 8
    T = _T if _T is not None else 512
    assert B * L == 8 * tokens

    def permd(v):
        return np.ascontiguousarray(
            np.asarray(v, np.float32)[PD].reshape(DT, 128).T)

    parf = {
        "ln1g": permd(L0_attn_ln1_g), "ln1b": permd(L0_attn_ln1_b),
        "ln2g": permd(L0_attn_ln2_g), "ln2b": permd(L0_attn_ln2_b),
        "mixk": permd(L0_attn_mix_k), "mixv": permd(L0_attn_mix_v),
        "mixr": permd(L0_attn_mix_r), "tdecay": permd(L0_attn_time_decay),
        "tfirst": permd(L0_attn_time_first),
    }
    wmat = {
        "wk": np.asarray(L0_attn_Wk, np.float32)[np.ix_(PD, PD)],
        "wv": np.asarray(L0_attn_Wv, np.float32)[np.ix_(PD, PD)],
        "wr": np.asarray(L0_attn_Wr, np.float32)[np.ix_(PD, PD)],
        "wo": np.asarray(L0_attn_Wo, np.float32)[np.ix_(PD, PD)],
        "wfk": np.asarray(L0_attn_Wfk, np.float32)[PD],
        "wfv": np.asarray(L0_attn_Wfv, np.float32)[:, PD],
        "wfr": np.asarray(L0_attn_Wfr, np.float32)[np.ix_(PD, PD)],
        "wgt": np.asarray(L0_ffn_w_gate, np.float32)[np.ix_(PH, PD)].T,
        "wut": np.asarray(L0_ffn_w_up, np.float32)[np.ix_(PH, PD)].T,
        "wdt": np.asarray(L0_ffn_w_down, np.float32)[np.ix_(PD, PH)].T,
    }
    wshs = []
    for c in range(8):
        sh = np.zeros((128, SHW), np.float32)
        for name, bpc, cols in SH_PLAIN:
            o = SH_OFF[name]
            kt = {"wfv": KT}.get(name, DT)
            r0 = c * bpc
            blk = wmat[name][r0 * 128:(r0 + bpc) * 128, :]
            sh[:, o:o + bpc * cols] = blk.reshape(bpc, 128, cols).transpose(
                1, 0, 2).reshape(128, bpc * cols)
        for name, kt, cols, shw in SH_Q:
            o = SH_OFF[name]
            wm = wmat[name]
            piece = wm[:, c * shw:(c + 1) * shw]
            sh[:, o:o + kt * shw] = piece.reshape(kt, 128, shw).transpose(
                1, 0, 2).reshape(128, kt * shw)
        wshs.append(np.ascontiguousarray(sh))

    nc = build_program(tokens=tokens, T=T)

    in_maps = []
    halves = L // tokens
    for c in range(8):
        b, h = c // halves, c % halves
        t0 = h * tokens
        xs = np.ascontiguousarray(x[b, t0:t0 + tokens, :][:, PD].T)
        if t0 > 0:
            xprev = permd(x[b, t0 - 1, :])
            xprev2 = permd(x[b, t0 - 2, :])
            flf = np.zeros((128, DT), np.float32)
            flc = np.ones((128, DT), np.float32)
            g0 = np.ones((128, 16), np.float32)
        else:
            xprev = np.zeros((128, DT), np.float32)
            xprev2 = np.zeros((128, DT), np.float32)
            flf = np.ones((128, DT), np.float32)
            flc = np.zeros((128, DT), np.float32)
            g0 = np.zeros((128, 16), np.float32)
        par = np.zeros((128, NPAR), np.float32)
        for name, col in PAR.items():
            if name in parf:
                par[:, col:col + DT] = parf[name]
        par[:, PAR["flfirst"]:PAR["flfirst"] + DT] = flf
        par[:, PAR["flcarry"]:PAR["flcarry"] + DT] = flc
        par[:, PAR["xprev"]:PAR["xprev"] + DT] = xprev
        par[:, PAR["xprev2"]:PAR["xprev2"] + DT] = xprev2
        in_maps.append({
            "xT": xs, "par": par, "gate0": g0, "wsh": wshs[c],
        })

    res = _RUN[0](nc, in_maps, core_ids=list(range(8)))

    out = np.empty((B, L, D), np.float32)
    for c in range(8):
        b, h = c // halves, c % halves
        t0 = h * tokens
        out[b, t0:t0 + tokens, :][:, PD] = np.asarray(
            res.results[c]["outT"], np.float32).T
    return out


# version 19 (final)
# speedup vs baseline: 1.1333x; 1.1333x over previous
import numpy as np
from contextlib import ExitStack

import concourse.bass as bass
import concourse.mybir as mybir
from concourse.bass_utils import run_bass_kernel_spmd

F32 = mybir.dt.float32
BF16 = mybir.dt.bfloat16
AF = mybir.ActivationFunctionType
ALU = mybir.AluOpType

D, DT = 1024, 8
KD, KT = 4096, 32
HD, HT = 2816, 22
EPS_LN = 1e-5
EPS_WKV = 1e-8

GROUPS_D = [(0, 8, [64])]
GROUPS_H = [(4 * i, 4, [64, 32]) for i in range(5)] + [(20, 2, [64, 32, 16])]

SH_PLAIN = [("wk", 1, D), ("wv", 1, D), ("wr", 1, D), ("wo", 1, D),
            ("wfr", 1, D), ("wfk", 1, KD), ("wfv", 4, D)]
SH_OFF = {}
_o = 0
for _n, _b, _c in SH_PLAIN:
    SH_OFF[_n] = _o
    _o += _b * _c
SH_Q = [("wgt", DT, HD, 352), ("wut", DT, HD, 352), ("wdt", HT, D, 128)]
for _n, _kt, _cols, _w in SH_Q:
    SH_OFF[_n] = _o
    _o += _kt * _w
SHW = _o

PAR_FIELDS = ["ln1g", "ln1b", "ln2g", "ln2b", "mixk", "mixv", "mixr",
              "tdecay", "tfirst", "flfirst", "flcarry", "xprev", "xprev2"]
PAR = {n: i * DT for i, n in enumerate(PAR_FIELDS)}
NPAR = len(PAR_FIELDS) * DT


def _perm_d():
    pd = np.empty(D, np.int64)
    for q in range(D):
        tau, p = q // 128, q % 128
        pd[q] = 16 * (p % 64) + 8 * (p // 64) + tau
    return pd


def _perm_h():
    ph = np.empty(HD, np.int64)
    for G in range(5):
        for gp in range(32):
            for ip in range(4):
                for tp in range(4):
                    ph[(4 * G + tp) * 128 + gp + 32 * ip] = (
                        512 * G + 16 * gp + 4 * ip + tp)
    for gp in range(16):
        for ip in range(8):
            for tp in range(2):
                ph[(20 + tp) * 128 + gp + 16 * ip] = 2560 + 16 * gp + 2 * ip + tp
    return ph


PD = _perm_d()
PH = _perm_h()


LABELS = {}
TAG = [""]
RQNC = [None]


class RQ:

    def __init__(self, name, serialize=True):
        self.name, self.n = name, 0
        self.serialize = serialize
        self.ops, self._pend = [], []
        self.sem = None

    def wait(self, q, cnt):
        if cnt > 0:
            self._pend.append((q, cnt))

    def emit(self, fn, inc=1):
        self.ops.append((self._pend, fn, inc, TAG[0]))
        self._pend = []
        self.n += inc
        return self.n

    def replay(self, eng):
        prev = 0
        for waits, fn, inc, tag in self.ops:
            for q, cnt in waits:
                eng.wait_ge(q.sem, cnt)
            if self.serialize and prev > 0:
                eng.wait_ge(self.sem, prev)
            i0 = RQNC[0].n_instructions() if RQNC[0] is not None else None
            inst = fn(eng)
            if i0 is not None:
                for k in range(i0, RQNC[0].n_instructions()):
                    LABELS[f"I-{k}"] = tag
            if inc:
                inst.then_inc(self.sem, inc)
            prev += inc


class Stream:

    def __init__(self, name):
        self.name, self.n = name, 0
        self.sem = None


class DmaEng:

    def __init__(self, name):
        self.name = name
        self.ops, self._pend = [], []

    def wait(self, q, cnt):
        if cnt > 0:
            self._pend.append((q, cnt))

    def dma(self, stream, out, in_, **kw):
        self.ops.append((self._pend, ("dma", stream, out, in_, kw)))
        self._pend = []
        stream.n += 16
        return stream.n

    def raw(self, fn, sem=None):
        self.ops.append((self._pend, ("raw", fn, sem)))
        self._pend = []

    def replay(self, eng):
        for waits, op in self.ops:
            for q, cnt in waits:
                eng.wait_ge(q.sem, cnt)
            if op[0] == "dma":
                _, stream, out, in_, kw = op
                eng.dma_start(out=out, in_=in_, **kw).then_inc(stream.sem, 16)
            else:
                _, fn, sem = op
                inst = fn(eng)
                if sem is not None:
                    inst.then_inc(sem.sem)


class SemRef:

    def __init__(self):
        self.sem = None


def build_program(tokens=4096, T=512):
    assert tokens % T == 0
    NCH = tokens // T
    nc = bass.Bass("TRN2", target_bir_lowering=False)
    ctx = ExitStack()

    xT = nc.dram_tensor("xT", [D, tokens], F32, kind="ExternalInput").ap()
    par_d = nc.dram_tensor("par", [128, NPAR], F32, kind="ExternalInput").ap()
    gt0_d = nc.dram_tensor("gate0", [128, 16], F32, kind="ExternalInput").ap()
    wsh = nc.dram_tensor("wsh", [128, SHW], F32, kind="ExternalInput").ap()
    outT = nc.dram_tensor("outT", [D, tokens], F32, kind="ExternalOutput").ap()
    wsc = {}
    for name, kt, cols in [("wk", DT, D), ("wv", DT, D), ("wr", DT, D),
                           ("wo", DT, D), ("wfk", DT, KD), ("wfv", KT, D),
                           ("wfr", DT, D), ("wgt", DT, HD), ("wut", DT, HD),
                           ("wdt", HT, D)]:
        wsc[name] = nc.dram_tensor("sc_" + name, [128, kt * cols], BF16).ap()
    snd = nc.dram_tensor("cc_snd", [128, 16], F32)
    rcv = nc.dram_tensor("cc_rcv", [256, 16], F32)
    AG1W = 2048
    snd_w1 = nc.dram_tensor("cc_sndw1", [128, AG1W], BF16)
    rcv_w1 = nc.dram_tensor("cc_rcvw1", [1024, AG1W], BF16, addr_space="Shared")
    snd_w2 = nc.dram_tensor("cc_sndw2", [128, SHW - AG1W], BF16)
    rcv_w2 = nc.dram_tensor("cc_rcvw2", [1024, SHW - AG1W], BF16,
                            addr_space="Shared")

    arena0 = (nc.sbuf_base + 31) & ~31
    off = [arena0]

    def fresh(name, shape, dtype):
        t = nc.alloc_sbuf_tensor_at(name, shape, dtype, offset=off[0])
        sz = int(np.prod(shape[1:])) * (4 if dtype == F32 else 2)
        off[0] += (sz + 31) & ~31
        return t

    def at(name, shape, dtype, o):
        return nc.alloc_sbuf_tensor_at(name, shape, dtype, offset=o)

    def ofs(t):
        return t.manual_sbuf_range[0]

    PARS = fresh("PARS", [128, NPAR], F32)
    GT0 = fresh("GT0", [128, 16], F32)
    AUXG = fresh("AUXG", [128, 16], F32)
    AUX = fresh("AUX", [128, 96], F32)
    ONESC = fresh("ONESC", [128, 1], BF16)
    SMALL = fresh("SMALL", [1, 3 * T + 32], F32)
    SMB = fresh("SMB", [1, 2 * T], BF16)
    ONESR = fresh("ONESR", [1, 128], BF16)
    HALO = fresh("HALO", [128, 7 * DT], BF16)
    CONST_VALS = [EPS_LN, EPS_WKV, -0.5, -0.25, -0.75, -1.25, -1.75, -2.5,
                  -3.5, -5.0, 0.5]
    CONSTS = fresh("CONSTS", [128, len(CONST_VALS)], F32)
    WB = [fresh("WB0", [128, 8192], BF16), fresh("WB1", [128, 8192], BF16)]
    trans0 = off[0]
    X = fresh("X", [128, DT * T], F32)
    X1 = fresh("X1", [128, DT * T], F32)
    XN = fresh("XN", [128, DT * (T + 1)], BF16)
    MIX = fresh("MIX", [128, DT * T], BF16)
    MIX2 = fresh("MIX2", [128, DT * T], BF16)
    DDb = fresh("DD", [128, DT * T], BF16)
    EK = fresh("EK", [128, DT * T], BF16)
    VV = fresh("VV", [128, DT * T], BF16)
    UU = fresh("UU", [128, DT * T], BF16)
    A8 = fresh("A8", [128, DT * T], BF16)
    B8 = fresh("B8", [128, DT * T], BF16)
    HH = fresh("HH", [128, HT * T], BF16)
    LQ1 = fresh("LQ1", [128, 2 * T], F32)
    LQ2 = fresh("LQ2", [128, 2 * T], BF16)
    LQ3 = fresh("LQ3", [128, 2 * T], BF16)
    LQ4 = fresh("LQ4", [128, 2 * T], BF16)
    AXG = fresh("AXG", [128, DT * T], F32)
    KRL = fresh("KRL", [128, 2 * T], BF16)
    TMQ1 = fresh("TMQ1", [128, T], F32)
    TMQ2 = fresh("TMQ2", [128, T], F32)
    RBq = fresh("RBq", [128, T], F32)
    SSq = fresh("SSq", [128, T], BF16)
    SCRA = fresh("SCRA", [128, T], F32)
    SCRB = fresh("SCRB", [128, T], F32)
    for _ci, _cv in enumerate(CONST_VALS):
        nc.const_aps.register(F32, _cv, CONSTS[:, _ci:_ci + 1])


    CSQ = at("CSQ", [128, DT * T], BF16, ofs(DDb))
    CXB = at("CXB", [128, DT * T], BF16, ofs(EK))
    KK = at("KK", [128, KT * T], BF16, ofs(A8))
    X3 = at("X3", [128, DT * T], F32, ofs(VV))
    WKV = at("WKV", [128, DT * T], BF16, ofs(DDb))
    RWKV = at("RWKV", [128, DT * T], BF16, ofs(MIX))
    RBW = at("RBW", [128, DT * T], BF16, ofs(VV))
    Rb = at("Rb", [128, DT * T], BF16, ofs(MIX2))
    YMIX = at("YMIX", [128, DT * T], BF16, ofs(UU))
    RRb = at("RRb", [128, DT * T], BF16, ofs(MIX))
    XQ = at("XQ", [128, DT * T], BF16, ofs(MIX2))
    HQ = at("HQ", [128, HT * T], BF16, ofs(HH))
    Fs = at("Fs", [128, DT * T], F32, ofs(A8))
    OUTb = at("OUTb", [128, DT * T], F32, ofs(VV))
    TMPF = at("TMPF", [128, DT * T], F32, ofs(MIX))
    SG_CH = [at(f"SG{i}", [128, 2 * T], BF16, o) for i, o in enumerate(
        [ofs(LQ2), ofs(LQ3), ofs(LQ4), ofs(TMQ1), ofs(SCRA), ofs(SCRB),
         ofs(KRL)])]
    SG8_CH = at("SG8", [128, 2 * T], BF16, ofs(TMQ2))
    YB_CH = at("YB", [128, 2 * T], F32, ofs(A8))
    SILS = at("SILS", [128, 2 * T], BF16, ofs(MIX))
    SGB_CH = [at(f"SGB{i}", [128, 2 * T], BF16, o) for i, o in enumerate(
        [ofs(B8) + 4 * T, ofs(B8) + 8 * T, ofs(B8) + 12 * T, ofs(A8) + 8 * T,
         ofs(MIX) + 4 * T, ofs(MIX) + 8 * T, ofs(MIX) + 12 * T])]
    SG8B_CH = at("SG8B", [128, 2 * T], BF16, ofs(A8) + 12 * T)
    KRL4 = at("KRL4", [128, 4 * T], BF16, ofs(LQ1))
    p0 = trans0
    WSH = at("WSH", [128, SHW], F32, p0 + 0)
    SNDB = at("SNDB", [128, SHW], BF16, p0 + 87040)
    qo = [p0 + 130560]

    def pre_at(name, shape, dtype):
        t = at(name, shape, dtype, qo[0])
        sz = int(np.prod(shape[1:])) * (4 if dtype == F32 else 2)
        qo[0] += (sz + 31) & ~31
        return t

    QW = 352
    QWP = 2 * QW
    QAX = pre_at("QAX", [128, DT * QW], F32)
    QL1 = pre_at("QL1", [128, QWP], F32)
    QY2 = pre_at("QY2", [128, QWP], F32)
    QL2 = pre_at("QL2", [128, QWP], BF16)
    QL3 = pre_at("QL3", [128, QWP], BF16)
    QL4 = pre_at("QL4", [128, QWP], BF16)
    QM1 = pre_at("QM1", [128, QW], F32)
    QM2 = pre_at("QM2", [128, QW], F32)
    QRB = pre_at("QRB", [128, QW], F32)
    QSS = pre_at("QSS", [128, QW], BF16)
    QSA = pre_at("QSA", [128, QW], F32)
    QSG6 = pre_at("QSG6", [128, QWP], BF16)
    QSG7 = pre_at("QSG7", [128, QWP], BF16)
    QSG8 = pre_at("QSG8", [128, QWP], BF16)
    PRELUDE_TOP = qo[0]
    SG_PRE = [QL2, QL3, QL4,
              at("QSG3", [128, 1024], BF16, ofs(QM1)),
              at("QSG4", [128, 1024], BF16, ofs(QM2)),
              QSG6, QSG7]
    SG8_PRE = QSG8
    PRELUDE_TOP = p0 + 160768
    top = max(off[0], PRELUDE_TOP)
    assert top <= arena0 + 212 * 1024, f"sbuf overflow {top}"
    nc.alloc_sbuf_tensor("arena_guard", [128, top - arena0], mybir.dt.uint8)


    A_CA, A_CB, A_TFE, A_EW, A_CA0, A_CB0, A_T1, A_T2 = 0, 8, 16, 24, 32, 40, 64, 72
    H_XNH, H_YNH, H_XNH2, H_XNS, H_YNS, H_SP, H_MRH = (
        0, 8, 16, 24, 32, 40, 48)

    psA = ctx.enter_context(nc.psum_tensor("psA", [128, T], F32))
    psB = ctx.enter_context(nc.psum_tensor("psB", [128, T], F32))
    psC = ctx.enter_context(nc.psum_tensor("psC", [128, T], F32))
    psDn = ctx.enter_context(nc.psum_tensor("psDn", [128, T], F32))
    psM = ctx.enter_context(nc.psum_tensor("psM", [128, T], F32))
    psR = ctx.enter_context(nc.psum_tensor("psR", [128, T], F32))
    psS = ctx.enter_context(nc.psum_tensor("psS", [1, T], F32))
    psQ = ctx.enter_context(nc.psum_tensor("psQ", [1, T], F32))

    qv, qs = RQ("v"), RQ("s")
    qt = RQ("t", serialize=False)
    dw, dp = DmaEng("w"), DmaEng("p")
    SX = Stream("sx")
    SWB = [Stream("swb0"), Stream("swb1")]
    SPL = [Stream("spl0"), Stream("spl1")]
    SPR = Stream("spr")
    SSC = Stream("ssc")
    SST = Stream("sst")
    SFD = Stream("sfd")
    SCC = Stream("scc")
    SWT = {n: Stream("sw_" + n) for n in
           ["wk", "wv", "wr", "wo", "wfr", "wfk", "wfv", "wgt", "wut", "wdt"]}
    POOL_STREAMS = [SSC, SST, SFD, SCC] + list(SWT.values())
    ccs = SemRef()

    def tile(mega, ti, w=None):
        w = T if w is None else w
        return mega[:, ti * T: ti * T + w]

    def v3(ap, a=DT):
        return ap.rearrange("p (a t) -> p a t", a=a)

    def mmg(ps, pairs):
        n = len(pairs)
        for i, (l, r) in enumerate(pairs):
            qt.emit(lambda t, l=l, r=r, s=(i == 0), e=(i == n - 1):
                    t.matmul(ps, l, r, start=s, stop=e),
                    inc=(1 if i == n - 1 else 0))
        return qt.n

    bank_free = {}

    def bank_wait(ps):
        if ps.name in bank_free:
            q, cnt = bank_free[ps.name]
            qt.wait(q, cnt)

    def bank_evac(ps, q, cnt):
        bank_free[ps.name] = (q, cnt)

    wb_state = [(None, 0), (None, 0)]
    wb_idx = [0]
    scratch_ready_ref = [{}]

    def load_granule(dram_ap, ntile=None, wname=None):
        i = wb_idx[0] % 2
        wb_idx[0] += 1
        q, cnt = wb_state[i]
        if q is not None:
            dw.wait(q, cnt)
        if wname is not None and wname in scratch_ready_ref[0]:
            dw.wait(*scratch_ready_ref[0][wname])
        if ntile is None:
            w = dram_ap.shape[1]
            dst = WB[i][:, 0:w]
        else:
            w = dram_ap.shape[1] * dram_ap.shape[2]
            dst = WB[i][:, 0:w].rearrange("p (a t) -> p a t", a=dram_ap.shape[1])
        dw.dma(SWB[i], dst, dram_ap)
        return i, (SWB[i], SWB[i].n)

    def note_wb(i):
        wb_state[i] = (qt, qt.n)

    def barrier(streams=None):
        counts = [(qv, qv.n), (qs, qs.n), (qt, qt.n)]
        pool = [(s, s.n) for s in
                (POOL_STREAMS if streams is None else streams)]
        for q, _ in counts:
            for oq, ocnt in counts:
                if oq is not q:
                    q.wait(oq, ocnt)
            for s, cnt in pool:
                q.wait(s, cnt)
        for oq, ocnt in counts:
            dp.wait(oq, ocnt)

    def emit_ln_sums(src, deps, bs=None, bq=None, tile_deps=None):
        bs = psS if bs is None else bs
        bq = psQ if bq is None else bq
        for q, c in deps:
            qs.wait(q, c)
            qv.wait(q, c)
        if tile_deps is None:
            qs.emit(lambda s: s.activation(CSQ[:], src[:], AF.Square))
            qv.emit(lambda v: v.tensor_copy(CXB[:], src[:]))
            m_sq, m_cx = qs.n, qv.n
            qt.wait(qs, m_sq)
            qt.wait(qv, m_cx)
            bank_wait(bs)
            mmg(bs[0:1, 0:T], [(ONESC[:], tile(CXB, ti)) for ti in range(DT)])
            bank_wait(bq)
            mmg(bq[0:1, 0:T], [(ONESC[:], tile(CSQ, ti)) for ti in range(DT)])
            return {"src": src, "m_sums": qt.n, "bs": bs, "bq": bq}
        bank_wait(bs)
        bank_wait(bq)
        for ti in range(DT):
            for q, c in tile_deps[ti]:
                qs.wait(q, c)
                qv.wait(q, c)
            qs.emit(lambda s, ti=ti: s.activation(
                tile(CSQ, ti), tile(src, ti), AF.Square))
            qv.emit(lambda v, ti=ti: v.tensor_copy(tile(CXB, ti),
                                                   tile(src, ti)))
            m_sq, m_cx = qs.n, qv.n
            qt.wait(qs, m_sq)
            qt.wait(qv, m_cx)
            st, en = ti == 0, ti == DT - 1
            qt.emit(lambda t, ti=ti, st=st, en=en: t.matmul(
                bs[0:1, 0:T], ONESC[:], tile(CXB, ti), start=st, stop=en),
                inc=(1 if en else 0))
            qt.emit(lambda t, ti=ti, st=st, en=en: t.matmul(
                bq[0:1, 0:T], ONESC[:], tile(CSQ, ti), start=st, stop=en),
                inc=(1 if en else 0))
        return {"src": src, "m_sums": qt.n, "bs": bs, "bq": bq}

    def emit_ln_finish(st, dst_bf=None, dst_f32=None, gcol=None, bcol=None,
                       halo_col=None):
        src, m_sums, bs, bq = st["src"], st["m_sums"], st["bs"], st["bq"]
        MN, VAR, TA = SMALL[:, 0:T], SMALL[:, T:2 * T], SMALL[:, 2 * T:3 * T]
        qv.wait(qt, m_sums)
        qv.emit(lambda v: v.tensor_scalar(MN, bs[0:1, 0:T], 1.0 / D, None, ALU.mult))
        qv.emit(lambda v: v.tensor_scalar(TA, bq[0:1, 0:T], 1.0 / D, None, ALU.mult))
        qv.emit(lambda v: v.tensor_tensor(VAR, MN, MN, ALU.mult))
        qv.emit(lambda v: v.tensor_tensor(VAR, TA, VAR, ALU.subtract))
        m_var = qv.n
        bank_evac(bs, qv, m_var)
        bank_evac(bq, qv, m_var)
        qs.wait(qv, m_var)
        qs.emit(lambda s: s.activation(TA, VAR, AF.Ln, bias=EPS_LN))
        qs.emit(lambda s: s.activation(VAR, TA, AF.Exp, scale=-0.5))
        m_rstd = qs.n
        MNB, RSB = SMB[:, 0:T], SMB[:, T:2 * T]
        qv.wait(qs, m_rstd)
        qv.emit(lambda v: v.tensor_copy(MNB, MN))
        qv.emit(lambda v: v.tensor_copy(RSB, VAR))
        m_sb = qv.n
        qt.wait(qv, m_sb)
        bank_wait(psM)
        mmg(psM[:], [(ONESR[:], MNB)])
        bank_wait(psR)
        mmg(psR[:], [(ONESR[:], RSB)])
        m_bc = qt.n
        Mb = psM[:].unsqueeze(1).broadcast_to([128, DT, T])
        Rc = psR[:].unsqueeze(1).broadcast_to([128, DT, T])
        qv.wait(qt, m_bc)
        qv.emit(lambda v: v.tensor_tensor(v3(TMPF[:]), v3(src[:]), Mb, ALU.subtract))
        m_src_dead = qv.n
        if dst_f32 is not None:
            qv.emit(lambda v: v.tensor_tensor(v3(dst_f32[:]), v3(TMPF[:]), Rc,
                                              ALU.mult))
            m_out = qv.n
            bank_evac(psM, qv, m_out)
            bank_evac(psR, qv, m_out)
            return m_out, m_src_dead
        qv.emit(lambda v: v.tensor_tensor(v3(TMPF[:]), v3(TMPF[:]), Rc, ALU.mult))
        m = qv.n
        bank_evac(psM, qv, m)
        bank_evac(psR, qv, m)
        hv = dst_bf[:].rearrange("p (a t) -> p a t", a=DT)
        hs = HALO[:, halo_col:halo_col + DT]
        qv.emit(lambda v: v.tensor_copy(hv[:, :, 0:1], hs.unsqueeze(2)))
        for ti in range(DT):
            gc = PARS[:, gcol + ti:gcol + ti + 1]
            bc = PARS[:, bcol + ti:bcol + ti + 1]
            dsl = dst_bf[:, ti * (T + 1) + 1: ti * (T + 1) + 1 + T]
            qv.emit(lambda v, a=dsl, b=tile(TMPF, ti), g=gc, c_=bc:
                    v.tensor_scalar(a, b, g, c_, ALU.mult, ALU.add))
        return qv.n, m_src_dead

    def emit_ln(src, deps, dst_bf=None, dst_f32=None, gcol=None, bcol=None,
                halo_col=None, tile_deps=None):
        st = emit_ln_sums(src, deps, tile_deps=tile_deps)
        return emit_ln_finish(st, dst_bf=dst_bf, dst_f32=dst_f32, gcol=gcol,
                              bcol=bcol, halo_col=halo_col)

    QS_CHUNK = {"ax": AXG, "m1": TMQ1, "m2": TMQ2, "sc": TMQ2, "rb": RBq,
                "ss": SSq, "sa": SCRA, "l1": LQ1, "l2": LQ2, "l3": LQ3,
                "l4": LQ4, "pe": 2 * T, "sg": SG_CH, "sg8": SG8_CH, "y2": YB_CH,
                "sgB": SGB_CH, "sg8B": SG8B_CH,
                "_lastsg": {0: 0, 1: 0}, "_lastmc": {0: 0, 1: 0}, "_gpi": [0]}
    QS_PRE = {"ax": QAX, "m1": QM1, "m2": QM2, "sc": QM2, "rb": QRB,
              "ss": QSS, "sa": QSA, "l1": QL1, "l2": QL2, "l3": QL3,
              "l4": QL4, "pe": 1024, "sg": SG_PRE, "sg8": SG8_PRE, "y2": QY2}

    def _quant_group(gsrc, gdst, nt, W, folds, deps, qcb, S):
        AXG = S["ax"]; TMQ1 = S["m1"]; TMQ2 = S["m2"]
        SCq = S["sc"]; RBq = S["rb"]; SSq = S["ss"]; SCRA = S["sa"]
        LQ1 = S["l1"]; LQ2 = S["l2"]; LQ3 = S["l3"]; LQ4 = S["l4"]
        ax = AXG[:, 0:nt * W]
        ax3 = ax.rearrange("p (k c) -> p k c", k=nt)
        for q, c in deps:
            qs.wait(q, c)
            qv.wait(q, c)
        qs.emit(lambda s, a=ax3, b=gsrc: s.activation(a, b, AF.Abs))
        m_ax = qs.n
        qv.wait(qs, m_ax)
        M1, M2 = TMQ1[:, 0:W], TMQ2[:, 0:W]
        sls = [ax[:, k * W:(k + 1) * W] for k in range(nt)]
        if nt == 8:
            qv.emit(lambda v, m=M1, a=sls[0], b=sls[1]: v.tensor_tensor(m, a, b, ALU.max))
            qv.emit(lambda v, m=M2, a=sls[2], b=sls[3]: v.tensor_tensor(m, a, b, ALU.max))
            qv.emit(lambda v, m=M1, a=M1, b=M2: v.tensor_tensor(m, a, b, ALU.max))
            qv.emit(lambda v, m=M2, a=sls[4], b=sls[5]: v.tensor_tensor(m, a, b, ALU.max))
            qv.emit(lambda v, m=M1, a=M1, b=M2: v.tensor_tensor(m, a, b, ALU.max))
            qv.emit(lambda v, m=M2, a=sls[6], b=sls[7]: v.tensor_tensor(m, a, b, ALU.max))
            qv.emit(lambda v, m=M1, a=M1, b=M2: v.tensor_tensor(m, a, b, ALU.max))
        elif nt == 4:
            qv.emit(lambda v, m=M1, a=sls[0], b=sls[1]: v.tensor_tensor(m, a, b, ALU.max))
            qv.emit(lambda v, m=M2, a=sls[2], b=sls[3]: v.tensor_tensor(m, a, b, ALU.max))
            qv.emit(lambda v, m=M1, a=M1, b=M2: v.tensor_tensor(m, a, b, ALU.max))
        else:
            qv.emit(lambda v, m=M1, a=sls[0], b=sls[1]: v.tensor_tensor(m, a, b, ALU.max))
        lim = 128
        for f in folds:
            if f >= 32:
                qv.emit(lambda v, f=f, m1=M1, m2=M2: v.tensor_copy(
                    m2[0:f, :], m1[f:2 * f, :]))
                qv.emit(lambda v, f=f, m1=M1, m2=M2: v.tensor_tensor(
                    m1[0:f, :], m1[0:f, :], m2[0:f, :], ALU.max))
            else:
                dp.wait(qv, qv.n)
                dp.dma(SFD, M2[0:16, :], M1[16:32, :])
                qv.wait(SFD, SFD.n)
                qv.emit(lambda v, m1=M1, m2=M2: v.tensor_tensor(
                    m1[0:16, :], m1[0:16, :], m2[0:16, :], ALU.max))
            lim = f
        SCs, RBs, SSs = SCq[:, 0:W], RBq[:, 0:W], SSq[:, 0:W]
        qv.emit(lambda v, L=lim, a=SCs, m=M1: v.tensor_scalar(
            a[0:L, :], m[0:L, :], 1.0 / 6.0, 1e-12, ALU.mult, ALU.max))
        m_sc = qv.n
        qs.wait(qv, m_sc)
        qs.emit(lambda s, L=lim, a=SCs, W=W: s.activation(
            SCRA[0:L, 0:W], a[0:L, :], AF.Ln))
        qs.emit(lambda s, L=lim, a=RBs, W=W: s.activation(
            a[0:L, :], SCRA[0:L, 0:W], AF.Exp, scale=-1.0))
        m_rb = qs.n
        qv.wait(qs, m_rb)
        qv.emit(lambda v, L=lim, a=SSs, b=SCs: v.tensor_scalar(
            a[0:L, :], b[0:L, :], 0.5, None, ALU.mult))
        lvl = lim
        while lvl < 128:
            if lvl == 16:
                dp.wait(qv, qv.n)
                dp.dma(SFD, RBs[16:32, :], RBs[0:16, :])
                dp.dma(SFD, SSs[16:32, :], SSs[0:16, :])
                qv.wait(SFD, SFD.n)
            else:
                qv.emit(lambda v, L=lvl, a=RBs: v.tensor_copy(
                    a[L:2 * L, :], a[0:L, :]))
                qv.emit(lambda v, L=lvl, a=SSs: v.tensor_copy(
                    a[L:2 * L, :], a[0:L, :]))
            lvl *= 2
        piece = max(1, min(nt, S["pe"] // W))
        YSLOT = [LQ1, S["y2"]]
        pieces = list(range(0, nt, piece))
        if "sgB" in S:
            SGsets = [S["sg"], S["sgB"]]
            SG8s = [S["sg8"], S["sg8B"]]
            lastsg = S["_lastsg"]
            lastmc = S["_lastmc"]

            def emit_C(md):
                par, k, Wp, p0, r3, ssb, sgd = md
                SG = SGsets[par]
                SG8 = SG8s[par]
                sl = [t[:, 0:Wp] for t in SG]
                qv.wait(qs, sgd)
                qv.emit(lambda v, a=sl[0], b=sl[1]: v.tensor_tensor(a, a, b, ALU.add))
                qv.emit(lambda v, a=sl[2], b=sl[3]: v.tensor_tensor(a, a, b, ALU.add))
                qv.emit(lambda v, a=sl[0], b=sl[2]: v.tensor_tensor(a, a, b, ALU.add))
                qv.emit(lambda v, a=sl[4], b=sl[5]: v.tensor_tensor(a, a, b, ALU.add))
                qv.emit(lambda v, a=sl[0], b=sl[1]: v.tensor_scalar(
                    b, a, 0.5, 6.0, ALU.mult, ALU.add))
                qv.emit(lambda v, a=sl[1], b=sl[4]: v.tensor_tensor(a, a, b, ALU.add))
                qv.emit(lambda v, a=sl[6], b=sl[1]: v.scalar_tensor_tensor(
                    b, a, 2.0, b, ALU.mult, ALU.add))
                qv.emit(lambda v, a=SG8[:, 0:Wp], b=sl[4], ss=ssb, r3=r3:
                        v.tensor_tensor(r3(b), r3(a), ss, ALU.mult))
                qv.emit(lambda v, a=sl[1], b=sl[4], d=gdst, p0=p0, k=k, r3=r3:
                        v.tensor_tensor(d[:, p0:p0 + k, :], r3(a), r3(b), ALU.mult))
                lastmc[par] = qv.n

            pend = None
            for p0 in pieces:
                k = min(piece, nt - p0)
                Wp = k * W
                par = S["_gpi"][0] % 2
                S["_gpi"][0] += 1
                y = YSLOT[par][:, 0:Wp]
                r3 = lambda a, k=k: a.rearrange("p (a c) -> p a c", a=k)
                axp = ax[:, p0 * W: p0 * W + Wp]
                rbb = RBs.unsqueeze(1).broadcast_to([128, k, W])
                ssb = SSs.unsqueeze(1).broadcast_to([128, k, W])
                qv.wait(qs, lastsg[par])
                qv.emit(lambda v, y=y, a=axp, rb=rbb, r3=r3: v.tensor_tensor(
                    r3(y), r3(a), rb, ALU.mult))
                m_y = qv.n
                qs.wait(qv, m_y)
                qs.wait(qv, lastmc[par])
                SG = SGsets[par]
                SG8 = SG8s[par]
                for i, th in enumerate([0.25, 0.75, 1.25, 1.75, 2.5, 3.5, 5.0]):
                    qs.emit(lambda s, i=i, th=th, y=y, Wp=Wp, SG=SG:
                            s.activation(SG[i][:, 0:Wp], y, AF.Sign, bias=-th))
                qs.emit(lambda s, a=SG8[:, 0:Wp], b=gsrc, p0=p0, k=k, r3=r3:
                        s.activation(r3(a), b[:, p0:p0 + k, :], AF.Sign))
                lastsg[par] = qs.n
                md = (par, k, Wp, p0, r3, ssb, qs.n)
                if pend is not None:
                    emit_C(pend)
                pend = md
            emit_C(pend)
            if qcb is not None:
                qcb(qv.n)
            return
        SG = S["sg"]
        SG8 = S["sg8"]
        sg_done = {}
        for pi, p0 in enumerate(pieces):
            k = min(piece, nt - p0)
            Wp = k * W
            y = YSLOT[pi % 2][:, 0:Wp]
            r3 = lambda a, k=k: a.rearrange("p (a c) -> p a c", a=k)
            axp = ax[:, p0 * W: p0 * W + Wp]
            rbb = RBs.unsqueeze(1).broadcast_to([128, k, W])
            ssb = SSs.unsqueeze(1).broadcast_to([128, k, W])
            qv.wait(qs, sg_done.get(pi - 2, 0))
            qv.emit(lambda v, y=y, a=axp, k=k, rb=rbb, r3=r3: v.tensor_tensor(
                r3(y), r3(a), rb, ALU.mult))
            m_y = qv.n
            qs.wait(qv, m_y)
            for i, th in enumerate([0.25, 0.75, 1.25, 1.75]):
                qs.emit(lambda s, i=i, th=th, y=y, Wp=Wp: s.activation(
                    SG[i][:, 0:Wp], y, AF.Sign, bias=-th))
            qs.emit(lambda s, a=SG8[:, 0:Wp], b=gsrc, p0=p0, k=k, r3=r3:
                    s.activation(r3(a), b[:, p0:p0 + k, :], AF.Sign))
            sg_done[pi] = qs.n
            qv.emit(lambda v, y=y, l=SG[4][:, 0:Wp]: v.tensor_scalar(
                l, y, 2.5, None, ALU.is_gt))
            qv.emit(lambda v, y=y, l=SG[5][:, 0:Wp]: v.tensor_scalar(
                l, y, 3.5, None, ALU.is_gt))
            qv.emit(lambda v, y=y, l=SG[6][:, 0:Wp]: v.tensor_scalar(
                l, y, 5.0, None, ALU.is_gt))
            sl = [t[:, 0:Wp] for t in SG]
            qv.wait(qs, sg_done[pi])
            qv.emit(lambda v, a=sl[0], b=sl[1]: v.tensor_tensor(a, a, b, ALU.add))
            qv.emit(lambda v, a=sl[2], b=sl[3]: v.tensor_tensor(a, a, b, ALU.add))
            qv.emit(lambda v, a=sl[0], b=sl[2]: v.tensor_tensor(a, a, b, ALU.add))
            qv.emit(lambda v, a=sl[4], b=sl[5]: v.tensor_tensor(a, a, b, ALU.add))
            qv.emit(lambda v, a=sl[0], b=sl[1]: v.tensor_scalar(
                b, a, 0.5, 2.0, ALU.mult, ALU.add))
            qv.emit(lambda v, a=sl[4], b=sl[1]: v.scalar_tensor_tensor(
                b, a, 2.0, b, ALU.mult, ALU.add))
            qv.emit(lambda v, a=sl[6], b=sl[1]: v.scalar_tensor_tensor(
                b, a, 4.0, b, ALU.mult, ALU.add))
            qv.emit(lambda v, a=SG8[:, 0:Wp], b=sl[4], k=k, ss=ssb, r3=r3:
                    v.tensor_tensor(r3(b), r3(a), ss, ALU.mult))
            qv.emit(lambda v, a=sl[1], b=sl[4], d=gdst, p0=p0, k=k, r3=r3:
                    v.tensor_tensor(d[:, p0:p0 + k, :], r3(a), r3(b), ALU.mult))
        if qcb is not None:
            qcb(qv.n)

    def emit_quant(src, src_stride, c0, W, groups, deps, dst, dst_stride,
                   dst_c0, dst_tile_base=None, group_cb=None, S=None):
        S = S if S is not None else QS_CHUNK
        for (t0, nt, folds) in groups:
            src_ap = src if isinstance(src, bass.AP) else src[:]
            dst_ap = dst if isinstance(dst, bass.AP) else dst[:]
            gsrc = src_ap.rearrange("p (k c) -> p k c", c=src_stride)[
                :, t0:t0 + nt, c0:c0 + W]
            dt0 = t0 if dst_tile_base is None else dst_tile_base
            gdst = dst_ap.rearrange("p (k c) -> p k c", c=dst_stride)[
                :, dt0:dt0 + nt, dst_c0:dst_c0 + W]
            qcb = None
            if group_cb is not None:
                qcb = (lambda m_v, t0=t0, nt=nt: group_cb(t0, nt, m_v))
            _quant_group(gsrc, gdst, nt, W, folds, deps, qcb, S)
        return qv.n, qs.n

    def emit_sigmoid(src_ap, dst_ap, w, deps=()):
        for q, c in deps:
            qs.wait(q, c)
        qs.emit(lambda s: s.activation(SCRA[:, 0:w], src_ap, AF.Exp, scale=-1.0))
        qs.emit(lambda s: s.activation(SCRB[:, 0:w], SCRA[:, 0:w], AF.Ln, bias=1.0))
        qs.emit(lambda s: s.activation(dst_ap, SCRB[:, 0:w], AF.Exp, scale=-1.0))
        return qs.n

    def emit_tiny_ln(src_off, dst_halo, gcol, bcol, gate_col):
        src = AUX[:, src_off:src_off + DT]
        qv.emit(lambda v: v.tensor_copy(HALO[:, H_SP:H_SP + DT], src))
        m_cast = qv.n
        qs.wait(qv, m_cast)
        qs.emit(lambda s: s.activation(KRL[:, 0:DT], src, AF.Square))
        m_sq = qs.n
        qt.wait(qv, m_cast)
        qt.wait(qs, m_sq)
        bank_wait(psS)
        mmg(psS[:, 0:DT], [(ONESC[:], HALO[:, H_SP:H_SP + DT])])
        bank_wait(psQ)
        mmg(psQ[:, 0:DT], [(ONESC[:], KRL[:, 0:DT])])
        m_mm = qt.n
        s1 = SMALL[:, 3 * T:3 * T + 1]
        s2 = SMALL[:, 3 * T + 1:3 * T + 2]
        s3 = SMALL[:, 3 * T + 2:3 * T + 3]
        qv.wait(qt, m_mm)
        qv.emit(lambda v: v.reduce_sum(s1, psS[:, 0:DT], axis=mybir.AxisListType.X))
        qv.emit(lambda v: v.reduce_sum(s2, psQ[:, 0:DT], axis=mybir.AxisListType.X))
        qv.emit(lambda v: v.tensor_scalar(s1, s1, 1.0 / D, None, ALU.mult))
        qv.emit(lambda v: v.tensor_scalar(s2, s2, 1.0 / D, None, ALU.mult))
        qv.emit(lambda v: v.tensor_tensor(s3, s1, s1, ALU.mult))
        qv.emit(lambda v: v.tensor_tensor(s2, s2, s3, ALU.subtract))
        m_var = qv.n
        bank_evac(psS, qv, m_var)
        bank_evac(psQ, qv, m_var)
        qs.wait(qv, m_var)
        qs.emit(lambda s: s.activation(s3, s2, AF.Ln, bias=EPS_LN))
        qs.emit(lambda s: s.activation(s2, s3, AF.Exp, scale=-0.5))
        m_rstd = qs.n
        qv.wait(qs, m_rstd)
        qv.emit(lambda v: v.tensor_copy(SMB[:, 0:1], s1))
        qv.emit(lambda v: v.tensor_copy(SMB[:, 1:2], s2))
        m_sb = qv.n
        qt.wait(qv, m_sb)
        bank_wait(psM)
        mmg(psM[:, 0:2], [(ONESR[:], SMB[:, 0:2])])
        m_bc = qt.n
        qv.wait(qt, m_bc)
        mb = psM[:, 0:1].broadcast_to([128, DT])
        rb = psM[:, 1:2].broadcast_to([128, DT])
        t1 = AUX[:, A_T1:A_T1 + DT]
        qv.emit(lambda v: v.tensor_tensor(t1, src, mb, ALU.subtract))
        qv.emit(lambda v: v.tensor_tensor(t1, t1, rb, ALU.mult))
        m = qv.n
        bank_evac(psM, qv, m)
        if gcol is not None:
            for ti in range(DT):
                qv.emit(lambda v, ti=ti: v.tensor_scalar(
                    t1[:, ti:ti + 1], t1[:, ti:ti + 1],
                    PARS[:, gcol + ti:gcol + ti + 1],
                    PARS[:, bcol + ti:bcol + ti + 1], ALU.mult, ALU.add))
        if gate_col is not None:
            qv.emit(lambda v: v.tensor_tensor(
                t1, t1, PARS[:, gate_col:gate_col + DT], ALU.mult))
        qv.emit(lambda v: v.tensor_copy(HALO[:, dst_halo:dst_halo + DT], t1))
        return qv.n


    dw.dma(SPR, PARS[:], par_d[:, :])
    dw.dma(SPR, GT0[:], gt0_d[:, :])
    m_par = SPR.n
    qv.wait(SPR, m_par)
    qv.emit(lambda v: v.memset(ONESC[:], 1.0))
    qv.emit(lambda v: v.memset(ONESR[:], 1.0))
    qv.emit(lambda v: v.memset(AUX[:], 0.0))
    qv.emit(lambda v: v.memset(HALO[:], 0.0))
    for _ci, _cv in enumerate(CONST_VALS):
        qv.emit(lambda v, ci=_ci, cv=_cv: v.memset(CONSTS[:, ci:ci + 1], cv))
    m_consts = qv.n
    qs.wait(SPR, m_par)
    qs.wait(qv, m_consts)
    qs.emit(lambda s: s.activation(SCRA[:, 0:DT],
                                   PARS[:, PAR["tdecay"]:PAR["tdecay"] + DT],
                                   AF.Exp))
    qs.emit(lambda s: s.activation(AUX[:, A_EW:A_EW + DT], SCRA[:, 0:DT],
                                   AF.Exp, scale=-1.0))
    m_ew = qs.n
    qv.wait(qs, m_ew)
    qv.emit(lambda v: v.tensor_tensor(
        AUX[:, A_TFE:A_TFE + DT], PARS[:, PAR["tfirst"]:PAR["tfirst"] + DT],
        PARS[:, PAR["flfirst"]:PAR["flfirst"] + DT], ALU.mult))
    qv.emit(lambda v: v.tensor_copy(AUX[:, A_T2:A_T2 + DT],
                                    PARS[:, PAR["xprev"]:PAR["xprev"] + DT]))
    emit_tiny_ln(A_T2, H_XNH, PAR["ln1g"], PAR["ln1b"], PAR["flcarry"])
    qv.emit(lambda v: v.tensor_copy(AUX[:, A_T2:A_T2 + DT],
                                    PARS[:, PAR["xprev2"]:PAR["xprev2"] + DT]))
    emit_tiny_ln(A_T2, H_XNH2, PAR["ln1g"], PAR["ln1b"], PAR["flcarry"])

    dw.wait(qt, qt.n)
    dw.wait(qs, qs.n)
    dw.wait(qv, qv.n)
    dw.dma(SPL[0], WSH[:, 0:AG1W], wsh[:, 0:AG1W])
    m_wsh1 = SPL[0].n
    dw.dma(SPL[1], WSH[:, AG1W:SHW], wsh[:, AG1W:SHW])
    m_wsh = SPL[1].n
    for name, bpc, cols in SH_PLAIN:
        o = SH_OFF[name]
        w = bpc * cols
        if o + w <= AG1W:
            qs.wait(SPL[0], m_wsh1)
        else:
            qs.wait(SPL[1], m_wsh)
        sc = 0.5 if name == "wfk" else None
        if sc is None:
            qs.emit(lambda s, o=o, w=w: s.activation(
                SNDB[:, o:o + w], WSH[:, o:o + w], AF.Copy))
        else:
            qs.emit(lambda s, o=o, w=w, sc=sc: s.activation(
                SNDB[:, o:o + w], WSH[:, o:o + w], AF.Copy, scale=sc))
        if name == "wv":
            dp.wait(qs, qs.n)
            dp.dma(SSC, snd_w1.ap(), SNDB[:, 0:AG1W])
            dp.wait(SSC, SSC.n)
            dp.raw(lambda g: g.collective_compute(
                "AllGather", ALU.bypass,
                replica_groups=[[0, 1, 2, 3, 4, 5, 6, 7]],
                ins=[snd_w1.ap().opt()], outs=[rcv_w1.ap().opt()]), sem=ccs)
    for name, kt, cols, shw in SH_Q:
        o = SH_OFF[name]
        groups = GROUPS_D if kt == DT else GROUPS_H
        emit_quant(WSH[:, o:o + kt * shw], shw, 0, shw, groups,
                   [(SPL[1], m_wsh)],
                   SNDB[:, o:o + kt * shw], shw, 0, S=QS_PRE)
    dp.wait(qs, qs.n)
    dp.wait(qv, qv.n)
    dp.dma(SSC, snd_w2.ap(), SNDB[:, AG1W:SHW])
    m_sndw = SSC.n
    dp.wait(SSC, SSC.n)
    dp.raw(lambda g: g.collective_compute(
        "AllGather", ALU.bypass,
        replica_groups=[[0, 1, 2, 3, 4, 5, 6, 7]],
        ins=[snd_w2.ap().opt()], outs=[rcv_w2.ap().opt()]), sem=ccs)
    scratch_ready = {}
    rcv1 = rcv_w1.ap()
    rcv3 = rcv_w2.ap()
    dp.wait(ccs, 1)
    for name, bpc, cols in SH_PLAIN:
        o = SH_OFF[name]
        w = bpc * cols
        if o + w > AG1W and name == "wr":
            dp.wait(ccs, 2)
        for r in range(8):
            src = (rcv1[r * 128:(r + 1) * 128, o:o + w] if o + w <= AG1W
                   else rcv3[r * 128:(r + 1) * 128, o - AG1W:o - AG1W + w])
            dp.dma(SWT[name], wsc[name][:, r * w:(r + 1) * w], src)
        scratch_ready[name] = (SWT[name], SWT[name].n)
    for name, kt, cols, shw in SH_Q:
        o = SH_OFF[name] - AG1W
        for r in range(8):
            dst3 = wsc[name].rearrange("p (k c) -> p k c", c=cols)[
                :, :, r * shw:(r + 1) * shw]
            src3 = rcv3[r * 128:(r + 1) * 128, o:o + kt * shw].rearrange(
                "p (k c) -> p k c", c=shw)
            dp.dma(SWT[name], dst3, src3)
        scratch_ready[name] = (SWT[name], SWT[name].n)
    scratch_ready_ref[0] = scratch_ready

    for q in (qv, qs, qt):
        for oq, ocnt in [(qv, qv.n), (qs, qs.n), (qt, qt.n)]:
            if oq is not q:
                q.wait(oq, ocnt)
        q.wait(SSC, m_sndw)
    dw.wait(qv, qv.n)
    dw.wait(qs, qs.n)
    dw.wait(SSC, m_sndw)

    EWc = lambda ti: AUX[:, A_EW + ti:A_EW + ti + 1]

    state = {"xload": 0, "store": 0, "m_fin_dead": 0}

    def x_load(c, tile_waits=None):
        for d in range(DT):
            if tile_waits is not None:
                dw.wait(qv, tile_waits[d])
            dw.dma(SX, X[:, d * T:(d + 1) * T],
                   xT[d * 128:(d + 1) * 128, c * T:(c + 1) * T])
        state["xload"] = SX.n

    def chunk_common_attn(c, pass2, first_chunk_carry0, early=None,
                          ln1_deps=()):
        T1 = T + 1
        if early is None:
            early = emit_ln_sums(X, [(SX, state["xload"])] + list(ln1_deps))
        m_ln1, _ = emit_ln_finish(early, dst_bf=XN,
                                  gcol=PAR["ln1g"], bcol=PAR["ln1b"],
                                  halo_col=(H_XNH if c == 0 else H_XNS))
        hvx = XN[:].rearrange("p (a t) -> p a t", a=DT)
        qv.emit(lambda v: v.tensor_copy(HALO[:, H_XNS:H_XNS + DT].unsqueeze(2),
                                        hvx[:, :, T1 - 1:T1]))
        cur = XN[:].rearrange("p (a t) -> p a t", a=DT)[:, :, 1:T1]
        prv = XN[:].rearrange("p (a t) -> p a t", a=DT)[:, :, 0:T]
        qv.emit(lambda v: v.tensor_tensor(v3(DDb[:]), cur, prv, ALU.subtract))
        m_dd = qv.n

        def mix_into(dstmega, mcol):
            for ti in range(DT):
                qv.emit(lambda v, ti=ti, d=dstmega, mc=mcol:
                        v.scalar_tensor_tensor(
                            tile(d, ti), tile(DDb, ti),
                            PARS[:, mc + ti:mc + ti + 1],
                            XN[:, ti * T1:ti * T1 + T], ALU.mult, ALU.add))
            return qv.n

        m_mk = mix_into(MIX, PAR["mixk"])
        bk, m_wk = load_granule(wsc["wk"], wname="wk")
        for d in range(DT):
            ps = psA if d % 2 == 0 else psB
            bank_wait(ps)
            qt.wait(qv, m_mk)
            qt.wait(*m_wk)
            mmg(ps[:], [(WB[bk][:, kk_ * D + d * 128: kk_ * D + (d + 1) * 128],
                         tile(MIX, kk_)) for kk_ in range(DT)])
            m_g = qt.n
            qs.wait(qt, m_g)
            if c == 0:
                qs.emit(lambda s, d=d, ps=ps: s.activation(
                    EK[:, d * T:d * T + 1], ps[:, 0:1], AF.Exp,
                    bias=AUX[:, A_TFE + d:A_TFE + d + 1]))
                qs.emit(lambda s, d=d, ps=ps: s.activation(
                    EK[:, d * T + 1:(d + 1) * T], ps[:, 1:T], AF.Exp))
            else:
                qs.emit(lambda s, d=d, ps=ps: s.activation(
                    tile(EK, d), ps[:], AF.Exp))
            bank_evac(ps, qs, qs.n)
        note_wb(bk)
        m_kmm = qt.n
        m_kepi = qs.n
        m_mv = mix_into(MIX2, PAR["mixv"])
        if pass2:
            qv.wait(qt, m_kmm)
            m_mr = mix_into(MIX, PAR["mixr"])
        bv, m_wv = load_granule(wsc["wv"], wname="wv")
        m_vept = [0] * DT
        for d in range(DT):
            ps = psA if d % 2 == 0 else psB
            bank_wait(ps)
            qt.wait(qv, m_mv)
            qt.wait(*m_wv)
            mmg(ps[:], [(WB[bv][:, kk_ * D + d * 128: kk_ * D + (d + 1) * 128],
                         tile(MIX2, kk_)) for kk_ in range(DT)])
            m_g = qt.n
            qs.wait(qt, m_g)
            qs.emit(lambda s, d=d, ps=ps: s.activation(
                tile(VV, d), ps[:], AF.Copy))
            m_vept[d] = qs.n
            bank_evac(ps, qs, qs.n)
        note_wb(bv)
        m_vepi = qs.n
        m_r_epi = None
        if pass2:
            m_mv = m_mr
            br, m_wr = load_granule(wsc["wr"], wname="wr")
            for d in range(DT):
                ps = psA if d % 2 == 0 else psB
                bank_wait(ps)
                qt.wait(qv, m_mr)
                qt.wait(*m_wr)
                mmg(ps[:], [(WB[br][:, kk_ * D + d * 128: kk_ * D + (d + 1) * 128],
                             tile(MIX, kk_)) for kk_ in range(DT)])
                m_g = qt.n
                emit_sigmoid(ps[:], tile(Rb, d), T, deps=[(qt, m_g)])
                bank_evac(ps, qs, qs.n)
            note_wb(br)
            m_r_epi = qs.n
        m_scant = [0] * DT
        for ti in range(DT):
            if first_chunk_carry0 and c == 0:
                init_a = 0.0
                init_b = 0.0
            elif c == 0:
                init_a = AUX[:, A_CA0 + ti:A_CA0 + ti + 1]
                init_b = AUX[:, A_CB0 + ti:A_CB0 + ti + 1]
            else:
                init_a = AUX[:, A_CA + ti:A_CA + ti + 1]
                init_b = AUX[:, A_CB + ti:A_CB + ti + 1]
            qv.wait(qs, m_vept[ti])
            qv.emit(lambda v, ti=ti: v.tensor_tensor(
                tile(UU, ti), tile(EK, ti), tile(VV, ti), ALU.mult))
            qv.emit(lambda v, ti=ti, ia=init_a: v.tensor_tensor_scan(
                tile(A8, ti), EWc(ti).broadcast_to([128, T]), tile(UU, ti),
                ia, ALU.mult, ALU.add))
            qv.emit(lambda v, ti=ti, ib=init_b: v.tensor_tensor_scan(
                tile(B8, ti), EWc(ti).broadcast_to([128, T]), tile(EK, ti),
                ib, ALU.mult, ALU.add))
            m_scant[ti] = qv.n
        av = A8[:].rearrange("p (a t) -> p a t", a=DT)
        bv8 = B8[:].rearrange("p (a t) -> p a t", a=DT)
        qv.emit(lambda v: v.tensor_copy(AUX[:, A_CA:A_CA + DT].unsqueeze(2),
                                        av[:, :, T - 1:T]))
        qv.emit(lambda v: v.tensor_copy(AUX[:, A_CB:A_CB + DT].unsqueeze(2),
                                        bv8[:, :, T - 1:T]))
        m_scan = qv.n
        return m_ln1, (m_scan, m_scant), m_r_epi, m_mv

    x_load(0)
    barrier(streams=[SSC, SST, SFD, SCC])
    prev_mv = 0
    for c in range(NCH):
        m_ln1, m_scan, _, m_mv = chunk_common_attn(
            c, pass2=False, first_chunk_carry0=True,
            ln1_deps=[(qv, prev_mv)])
        prev_mv = m_mv
        if c + 1 < NCH:
            dw.wait(qv, m_ln1)
            x_load(c + 1)

    dp.wait(qv, qv.n)
    dp.dma(SCC, snd.ap(), AUX[:, 0:16])
    dp.wait(SCC, SCC.n)
    dp.raw(lambda g: g.collective_compute(
        "AllGather", ALU.bypass,
        replica_groups=[[0, 1], [2, 3], [4, 5], [6, 7]],
        ins=[snd.ap().opt()], outs=[rcv.ap().opt()]), sem=ccs)
    dp.wait(ccs, 3)
    dp.dma(SCC, AUXG[:], rcv.ap()[0:128, :])
    m_ccd = SCC.n
    qv.wait(SCC, m_ccd)
    qv.emit(lambda v: v.tensor_tensor(AUX[:, A_CA0:A_CA0 + 16], AUXG[:],
                                      GT0[:], ALU.mult))
    m_cu = qv.n

    qs.wait(qv, m_cu)
    qs.emit(lambda s: s.activation(SCRA[:, 0:DT], AUX[:, A_CB0:A_CB0 + DT],
                                   AF.Ln, bias=EPS_WKV))
    qs.emit(lambda s: s.activation(SCRA[:, 8:8 + DT], SCRA[:, 0:DT],
                                   AF.Exp, scale=-1.0))
    m_rb = qs.n
    qv.wait(qs, m_rb)
    qv.emit(lambda v: v.tensor_tensor(AUX[:, A_T2:A_T2 + DT],
                                      AUX[:, A_CA0:A_CA0 + DT],
                                      SCRA[:, 8:8 + DT], ALU.mult))
    qv.emit(lambda v: v.tensor_tensor(HALO[:, H_SP:H_SP + DT],
                                      HALO[:, H_XNH:H_XNH + DT],
                                      HALO[:, H_XNH2:H_XNH2 + DT],
                                      ALU.subtract))
    for ti in range(DT):
        qv.emit(lambda v, ti=ti: v.scalar_tensor_tensor(
            HALO[:, H_MRH + ti:H_MRH + ti + 1],
            HALO[:, H_SP + ti:H_SP + ti + 1],
            PARS[:, PAR["mixr"] + ti:PAR["mixr"] + ti + 1],
            HALO[:, H_XNH2 + ti:H_XNH2 + ti + 1], ALU.mult, ALU.add))
    m_mrh = qv.n
    br, m_wr = load_granule(wsc["wr"], wname="wr")
    bank_wait(psA)
    for d in range(DT):
        qt.wait(qv, m_mrh)
        qt.wait(*m_wr)
        mmg(psA[:, d:d + 1],
            [(WB[br][:, kk_ * D + d * 128: kk_ * D + (d + 1) * 128],
              HALO[:, H_MRH + kk_:H_MRH + kk_ + 1]) for kk_ in range(DT)])
    note_wb(br)
    m_rh_mm = qt.n
    emit_sigmoid(psA[:, 0:DT], SCRB[:, 0:DT], DT, deps=[(qt, m_rh_mm)])
    m_rh = qs.n
    bank_evac(psA, qs, m_rh)
    qv.wait(qs, m_rh)
    qv.emit(lambda v: v.tensor_tensor(HALO[:, H_SP:H_SP + DT],
                                      SCRB[:, 0:DT], AUX[:, A_T2:A_T2 + DT],
                                      ALU.mult))
    m_rwh = qv.n
    bo, m_wo = load_granule(wsc["wo"], wname="wo")
    bank_wait(psB)
    for d in range(DT):
        qt.wait(qv, m_rwh)
        qt.wait(*m_wo)
        mmg(psB[:, d:d + 1],
            [(WB[bo][:, kk_ * D + d * 128: kk_ * D + (d + 1) * 128],
              HALO[:, H_SP + kk_:H_SP + kk_ + 1]) for kk_ in range(DT)])
    note_wb(bo)
    m_oh_mm = qt.n
    qv.wait(qt, m_oh_mm)
    qv.emit(lambda v: v.tensor_tensor(AUX[:, A_T2:A_T2 + DT],
                                      PARS[:, PAR["xprev"]:PAR["xprev"] + DT],
                                      psB[:, 0:DT], ALU.add))
    bank_evac(psB, qv, qv.n)
    emit_tiny_ln(A_T2, H_YNH, PAR["ln2g"], PAR["ln2b"], PAR["flcarry"])

    dw.wait(qv, qv.n)
    x_load(0)

    barrier()
    early_next = [None]
    for c in range(NCH):
        qv.wait(SST, state["store"])
        qs.wait(SST, state["store"])
        TAG[0] = f"p2c{c}"
        m_ln1, (m_scan, m_scant), m_r_epi, _ = chunk_common_attn(
            c, pass2=True, first_chunk_carry0=False, early=early_next[0])
        for ti in range(DT):
            qs.wait(qv, m_scant[ti])
            qs.emit(lambda s, ti=ti: s.activation(
                SCRA[:], tile(B8, ti), AF.Ln, bias=EPS_WKV))
            qs.emit(lambda s, ti=ti: s.activation(
                tile(RBW, ti), SCRA[:], AF.Exp, scale=-1.0))
        m_rbw = qs.n
        qv.wait(qs, m_rbw)
        qv.emit(lambda v: v.tensor_tensor(WKV[:], A8[:], RBW[:], ALU.mult))
        qv.wait(qs, m_r_epi)
        qv.emit(lambda v: v.tensor_tensor(RWKV[:], WKV[:], Rb[:], ALU.mult))
        m_rwkv = qv.n
        bo, m_wo = load_granule(wsc["wo"], wname="wo")
        m_x1t = [0] * DT
        for d in range(DT):
            ps = psA if d % 2 == 0 else psB
            bank_wait(ps)
            qt.wait(qv, m_rwkv)
            qt.wait(*m_wo)
            mmg(ps[:], [(WB[bo][:, kk_ * D + d * 128: kk_ * D + (d + 1) * 128],
                         tile(RWKV, kk_)) for kk_ in range(DT)])
            m_g = qt.n
            qv.wait(qt, m_g)
            qv.emit(lambda v, d=d, ps=ps: v.tensor_tensor(
                tile(X1, d), tile(X, d), ps[:], ALU.add))
            m_x1t[d] = qv.n
            bank_evac(ps, qv, qv.n)
        note_wb(bo)
        m_x1 = qv.n
        TAG[0] = TAG[0].split("-")[0] + "-ln2"
        m_ln2, _ = emit_ln(X1, [], dst_bf=XN,
                           gcol=PAR["ln2g"], bcol=PAR["ln2b"],
                           halo_col=(H_YNH if c == 0 else H_YNS),
                           tile_deps=[[(qv, m_x1t[d])] for d in range(DT)])
        T1 = T + 1
        hvy = XN[:].rearrange("p (a t) -> p a t", a=DT)
        qv.emit(lambda v: v.tensor_copy(HALO[:, H_YNS:H_YNS + DT].unsqueeze(2),
                                        hvy[:, :, T1 - 1:T1]))
        cur = hvy[:, :, 1:T1]
        prv = hvy[:, :, 0:T]
        qv.emit(lambda v: v.tensor_tensor(v3(YMIX[:]), cur, prv, ALU.add))
        m_ym = qv.n
        sq_done = {}
        for s_ in range(4):
            bf, m_wf = load_granule(
                wsc["wfk"].rearrange("p (k c) -> p k c", c=KD)[
                    :, :, s_ * 1024:(s_ + 1) * 1024], ntile=DT, wname="wfk")
            for dl in range(8):
                dg = s_ * 8 + dl
                ps = (psA, psB, psC, psDn)[dg % 4]
                bank_wait(ps)
                qt.wait(qv, m_ym)
                qt.wait(*m_wf)
                mmg(ps[:], [(WB[bf][:, kk_ * 1024 + dl * 128:
                                   kk_ * 1024 + (dl + 1) * 128],
                             tile(YMIX, kk_)) for kk_ in range(DT)])
                m_g = qt.n
                krl = KRL4[:, (dg % 4) * T:(dg % 4) * T + T]
                qs.wait(qt, m_g)
                qs.wait(qv, sq_done.get(dg - 4, 0))
                qs.emit(lambda s, ps=ps, k=krl: s.activation(k, ps[:], AF.Relu))
                m_rl = qs.n
                bank_evac(ps, qs, qs.n)
                qv.wait(qs, m_rl)
                qv.emit(lambda v, k=krl, dg=dg: v.tensor_tensor(
                    tile(KK, dg), k, k, ALU.mult))
                sq_done[dg] = qv.n
            note_wb(bf)
        m_kk = (qv, qv.n)
        ync = [XN[:, ti * T1 + 1: ti * T1 + 1 + T] for ti in range(DT)]
        bfr, m_wfr = load_granule(wsc["wfr"], wname="wfr")
        for d in range(DT):
            ps = psC if d % 2 == 0 else psDn
            bank_wait(ps)
            qt.wait(qv, m_ln2)
            qt.wait(*m_wfr)
            mmg(ps[:], [(WB[bfr][:, kk_ * D + d * 128: kk_ * D + (d + 1) * 128],
                         ync[kk_]) for kk_ in range(DT)])
            m_g = qt.n
            emit_sigmoid(ps[:], tile(RRb, d), T, deps=[(qt, m_g), (qv, m_rwkv)])
            bank_evac(ps, qs, qs.n)
        note_wb(bfr)
        m_rr = qs.n
        m_x3t = [0] * DT
        for s_ in range(4):
            bfv, m_wv2 = load_granule(
                wsc["wfv"].rearrange("p (k c) -> p k c", c=D)[
                    :, :, s_ * 256:(s_ + 1) * 256], ntile=KT, wname="wfv")
            for dl in range(2):
                d = s_ * 2 + dl
                ps = psA if d % 2 == 0 else psB
                bank_wait(ps)
                qt.wait(*m_kk)
                qt.wait(*m_wv2)
                mmg(ps[:], [(WB[bfv][:, kk_ * 256 + dl * 128:
                                    kk_ * 256 + (dl + 1) * 128],
                             tile(KK, kk_)) for kk_ in range(KT)])
                m_g = qt.n
                qv.wait(qt, m_g)
                qv.wait(qs, m_rr)
                qv.emit(lambda v, d=d, ps=ps: v.tensor_tensor(
                    SCRA[:], tile(RRb, d), ps[:], ALU.mult))
                qv.emit(lambda v, d=d: v.tensor_tensor(
                    SCRB[:], SCRA[:], tile(X1, d), ALU.add))
                qv.emit(lambda v, d=d: v.tensor_tensor(
                    tile(X3, d), SCRB[:], tile(X, d), ALU.add))
                m_x3t[d] = qv.n
                bank_evac(ps, qv, qv.n)
            note_wb(bfv)
        m_s = qv.n
        if c + 1 < NCH:
            x_load(c + 1, tile_waits=m_x3t)
        TAG[0] = f"p2c{c}-postln"
        m_x3, _ = emit_ln(X3, [(qv, m_s)], dst_f32=X3)
        if c + 1 < NCH:
            TAG[0] = f"p2c{c}-early"
            early_next[0] = emit_ln_sums(X, [(SX, state["xload"])],
                                         bs=psC, bq=psDn)
        else:
            early_next[0] = None
        m_xq, m_xq_s = emit_quant(X3, T, 0, T, GROUPS_D,
                                  [(qv, m_x3)], XQ, T, 0)
        silu_last = [0, 0]
        slabs = [(0, 8), (8, 8), (16, 6)]
        for j0, jw in slabs:
            bg, m_wg = load_granule(
                wsc["wgt"].rearrange("p (k c) -> p k c", c=HD)[
                    :, :, j0 * 128:(j0 + jw) * 128], ntile=DT, wname="wgt")
            bu, m_wu = load_granule(
                wsc["wut"].rearrange("p (k c) -> p k c", c=HD)[
                    :, :, j0 * 128:(j0 + jw) * 128], ntile=DT, wname="wut")
            for jl in range(jw):
                j = j0 + jl
                psG = psA if j % 2 == 0 else psM
                psU = psB if j % 2 == 0 else psR
                bank_wait(psG)
                qt.wait(qv, m_xq)
                qt.wait(qs, m_xq_s)
                qt.wait(*m_wg)
                mmg(psG[:], [(WB[bg][:, kk_ * jw * 128 + jl * 128:
                                    kk_ * jw * 128 + (jl + 1) * 128],
                              tile(XQ, kk_)) for kk_ in range(DT)])
                bank_wait(psU)
                qt.wait(*m_wu)
                mmg(psU[:], [(WB[bu][:, kk_ * jw * 128 + jl * 128:
                                    kk_ * jw * 128 + (jl + 1) * 128],
                              tile(XQ, kk_)) for kk_ in range(DT)])
                m_g = qt.n
                sil = SILS[:, (j % 2) * T:(j % 2) * T + T]
                qs.wait(qt, m_g)
                qs.wait(qv, silu_last[j % 2])
                qs.emit(lambda s, ps=psG, a=sil: s.activation(a, ps[:], AF.Silu))
                m_sig = qs.n
                bank_evac(psG, qs, m_sig)
                qv.wait(qs, m_sig)
                qv.emit(lambda v, ps=psU, j=j, a=sil: v.tensor_tensor(
                    tile(HH, j), a, ps[:], ALU.mult))
                bank_evac(psU, qv, qv.n)
                silu_last[j % 2] = qv.n
                m_after = qv.n
                for (t0g, ntg, foldsg) in GROUPS_H:
                    if t0g + ntg - 1 == j:
                        emit_quant(HH, T, 0, T, [(t0g, ntg, foldsg)],
                                   [(qv, m_after)], HQ, T, 0)
            note_wb(bg)
            note_wb(bu)
        m_hq, m_hq_s = qv.n, qs.n
        m_ft = [0] * DT
        for s_ in range(4):
            bd, m_wd = load_granule(
                wsc["wdt"].rearrange("p (k c) -> p k c", c=D)[
                    :, :, s_ * 256:(s_ + 1) * 256], ntile=HT, wname="wdt")
            for dl in range(2):
                d = s_ * 2 + dl
                ps = psA if d % 2 == 0 else psB
                bank_wait(ps)
                qt.wait(qv, m_hq)
                qt.wait(qs, m_hq_s)
                qt.wait(*m_wd)
                mmg(ps[:], [(WB[bd][:, kk_ * 256 + dl * 128:
                                   kk_ * 256 + (dl + 1) * 128],
                             tile(HQ, kk_)) for kk_ in range(HT)])
                m_g = qt.n
                qv.wait(qt, m_g)
                qv.emit(lambda v, d=d, ps=ps: v.tensor_tensor(
                    tile(Fs, d), tile(X3, d), ps[:], ALU.add))
                m_ft[d] = qv.n
                bank_evac(ps, qv, qv.n)
            note_wb(bd)
        m_f = qv.n
        fdeps = []
        if early_next[0] is not None:
            fdeps.append((qt, early_next[0]["m_sums"]))
        TAG[0] = f"p2c{c}-final"
        m_out, m_f_dead = emit_ln(Fs, fdeps, dst_f32=OUTb,
                                  tile_deps=[[(qv, m_ft[d])]
                                             for d in range(DT)])
        dp.wait(qv, m_out)
        dp.dma(SST, outT.rearrange("(a p) n -> p a n", p=128)[:, :, c * T:(c + 1) * T],
               OUTb[:].rearrange("p (a t) -> p a t", a=DT))
        state["store"] = SST.n

    for s in POOL_STREAMS:
        dp.wait(s, s.n)
    dp.raw(lambda g: g.engine_nop())

    RQNC[0] = nc
    with ExitStack() as bctx:
        for q in (qv, qs, qt):
            q.sem = bctx.enter_context(nc.semaphore("sem_" + q.name))
        for s in [SX, SPR, SSC, SST, SFD, SCC] + SWB + SPL + list(SWT.values()):
            s.sem = bctx.enter_context(nc.semaphore("sem_" + s.name))
        ccs.sem = bctx.enter_context(nc.semaphore("sem_cc"))
        block = bctx.enter_context(nc.Block())

        @block.sync
        def _(eng):
            dw.replay(eng)

        @block.gpsimd
        def _(eng):
            dp.replay(eng)

        @block.vector
        def _(eng):
            qv.replay(eng)

        @block.scalar
        def _(eng):
            qs.replay(eng)

        @block.tensor
        def _(eng):
            qt.replay(eng)

    return nc


_RUN = [run_bass_kernel_spmd]


def kernel(L0_attn_x, L0_attn_ln1_g, L0_attn_ln1_b, L0_attn_Wk, L0_attn_Wv,
           L0_attn_Wr, L0_attn_Wo, L0_attn_time_decay, L0_attn_time_first,
           L0_attn_mix_k, L0_attn_mix_v, L0_attn_mix_r, L0_attn_ln2_g,
           L0_attn_ln2_b, L0_attn_Wfk, L0_attn_Wfv, L0_attn_Wfr,
           L0_ffn_w_gate, L0_ffn_w_up, L0_ffn_w_down, _tokens=None, _T=None):
    x = np.asarray(L0_attn_x, np.float32)
    B, L, _ = x.shape
    tokens = _tokens if _tokens is not None else (B * L) // 8
    T = _T if _T is not None else 512
    assert B * L == 8 * tokens

    def permd(v):
        return np.ascontiguousarray(
            np.asarray(v, np.float32)[PD].reshape(DT, 128).T)

    parf = {
        "ln1g": permd(L0_attn_ln1_g), "ln1b": permd(L0_attn_ln1_b),
        "ln2g": permd(L0_attn_ln2_g), "ln2b": permd(L0_attn_ln2_b),
        "mixk": permd(L0_attn_mix_k), "mixv": permd(L0_attn_mix_v),
        "mixr": permd(L0_attn_mix_r), "tdecay": permd(L0_attn_time_decay),
        "tfirst": permd(L0_attn_time_first),
    }
    wmat = {
        "wk": np.asarray(L0_attn_Wk, np.float32)[np.ix_(PD, PD)],
        "wv": np.asarray(L0_attn_Wv, np.float32)[np.ix_(PD, PD)],
        "wr": np.asarray(L0_attn_Wr, np.float32)[np.ix_(PD, PD)],
        "wo": np.asarray(L0_attn_Wo, np.float32)[np.ix_(PD, PD)],
        "wfk": np.asarray(L0_attn_Wfk, np.float32)[PD],
        "wfv": np.asarray(L0_attn_Wfv, np.float32)[:, PD],
        "wfr": np.asarray(L0_attn_Wfr, np.float32)[np.ix_(PD, PD)],
        "wgt": np.asarray(L0_ffn_w_gate, np.float32)[np.ix_(PH, PD)].T,
        "wut": np.asarray(L0_ffn_w_up, np.float32)[np.ix_(PH, PD)].T,
        "wdt": np.asarray(L0_ffn_w_down, np.float32)[np.ix_(PD, PH)].T,
    }
    wshs = []
    for c in range(8):
        sh = np.zeros((128, SHW), np.float32)
        for name, bpc, cols in SH_PLAIN:
            o = SH_OFF[name]
            kt = {"wfv": KT}.get(name, DT)
            r0 = c * bpc
            blk = wmat[name][r0 * 128:(r0 + bpc) * 128, :]
            sh[:, o:o + bpc * cols] = blk.reshape(bpc, 128, cols).transpose(
                1, 0, 2).reshape(128, bpc * cols)
        for name, kt, cols, shw in SH_Q:
            o = SH_OFF[name]
            wm = wmat[name]
            piece = wm[:, c * shw:(c + 1) * shw]
            sh[:, o:o + kt * shw] = piece.reshape(kt, 128, shw).transpose(
                1, 0, 2).reshape(128, kt * shw)
        wshs.append(np.ascontiguousarray(sh))

    nc = build_program(tokens=tokens, T=T)

    in_maps = []
    halves = L // tokens
    for c in range(8):
        b, h = c // halves, c % halves
        t0 = h * tokens
        xs = np.ascontiguousarray(x[b, t0:t0 + tokens, :][:, PD].T)
        if t0 > 0:
            xprev = permd(x[b, t0 - 1, :])
            xprev2 = permd(x[b, t0 - 2, :])
            flf = np.zeros((128, DT), np.float32)
            flc = np.ones((128, DT), np.float32)
            g0 = np.ones((128, 16), np.float32)
        else:
            xprev = np.zeros((128, DT), np.float32)
            xprev2 = np.zeros((128, DT), np.float32)
            flf = np.ones((128, DT), np.float32)
            flc = np.zeros((128, DT), np.float32)
            g0 = np.zeros((128, 16), np.float32)
        par = np.zeros((128, NPAR), np.float32)
        for name, col in PAR.items():
            if name in parf:
                par[:, col:col + DT] = parf[name]
        par[:, PAR["flfirst"]:PAR["flfirst"] + DT] = flf
        par[:, PAR["flcarry"]:PAR["flcarry"] + DT] = flc
        par[:, PAR["xprev"]:PAR["xprev"] + DT] = xprev
        par[:, PAR["xprev2"]:PAR["xprev2"] + DT] = xprev2
        in_maps.append({
            "xT": xs, "par": par, "gate0": g0, "wsh": wshs[c],
        })

    res = _RUN[0](nc, in_maps, core_ids=list(range(8)))

    out = np.empty((B, L, D), np.float32)
    for c in range(8):
        b, h = c // halves, c % halves
        t0 = h * tokens
        out[b, t0:t0 + tokens, :][:, PD] = np.asarray(
            res.results[c]["outT"], np.float32).T
    return out
